# version 04
# baseline (speedup 1.0000x reference)
import contextlib
import numpy as np
import ml_dtypes
import concourse.bass as bass
import concourse.mybir as mybir
from concourse.bass_utils import run_bass_kernel_spmd

F32 = mybir.dt.float32
BF16 = mybir.dt.bfloat16
AF = mybir.ActivationFunctionType
ALU = mybir.AluOpType
AX = mybir.AxisListType
BF = ml_dtypes.bfloat16

ENGINES = ["sync", "tensor", "vector", "scalar", "gpsimd"]

DM = 1024
T = 4096
KC = DM // 128
NTILE = T // 128


class SeqProg:

    SEM_LIMIT = 20000

    def __init__(self, nc):
        self.nc = nc
        self.sems = [nc.alloc_semaphore("seq0")]
        self.base = 0
        self.total = 0
        self.carry = []
        self.items = []
        self.dsem = {e: nc.alloc_semaphore("d_" + e) for e in ENGINES}
        self.dcnt = {e: 0 for e in ENGINES}
        self.dwaited = {e: 0 for e in ENGINES}
        self.dep_carry = {e: None for e in ENGINES}
        self.dgen = {e: 0 for e in ENGINES}

    def _dep_rotate(self, eng):
        if self.dcnt[eng] > 50000:
            self.carry.append((self.dsem[eng], self.dcnt[eng]))
            self.dep_carry[eng] = (self.dsem[eng], self.dcnt[eng])
            self.dgen[eng] += 1
            self.dsem[eng] = self.nc.alloc_semaphore(f"d_{eng}{self.dgen[eng]}")
            self.dcnt[eng] = 0
            self.dwaited[eng] = 0

    def _rotate_if_needed(self):
        if self.total - self.base > self.SEM_LIMIT:
            self.carry.append((self.sems[-1], self.total - self.base))
            self.sems.append(self.nc.alloc_semaphore(f"seq{len(self.sems)}"))
            self.base = self.total

    def _pre(self):
        self._rotate_if_needed()
        waits = list(self.carry)
        self.carry = []
        if self.total > self.base:
            waits.append((self.sems[-1], self.total - self.base))
        for e in ENGINES:
            if self.dcnt[e] > self.dwaited[e]:
                waits.append((self.dsem[e], self.dcnt[e]))
                self.dwaited[e] = self.dcnt[e]
        return waits

    def op(self, eng, fn, inc=1):
        waits = self._pre()
        self.items.append((eng, waits, fn, (self.sems[-1], inc)))
        self.total += inc
        return (eng, self.sems[-1], self.total - self.base)

    def dma(self, eng, fn):
        return self.op(eng, fn, inc=16)

    def chain(self, eng, fn_incs):
        waits = self._pre()
        for i, (fn, inc) in enumerate(fn_incs):
            self.items.append((eng, waits if i == 0 else [], fn, (self.sems[-1], inc)))
            self.total += inc
        return (eng, self.sems[-1], self.total - self.base)

    def mm_chain(self, fns):
        return self.chain("tensor", [(f, 0) for f in fns[:-1]] + [(fns[-1], 1)])

    def dma_chain(self, fns):
        return self.chain("sync", [(f, 16) for f in fns])

    def _dep_waits(self, eng, deps):
        self._rotate_if_needed()
        waits = {}
        for d in deps:
            if d is None:
                continue
            (e2, s2, v2) = d
            if e2 == eng:
                continue
            if s2 in waits:
                waits[s2] = max(waits[s2], v2)
            else:
                waits[s2] = v2
        return [(s, v) for (s, v) in waits.items()]

    serial_mode = False

    def op_d(self, eng, fn, deps, inc=1):
        if self.serial_mode:
            return self.op(eng, fn, inc)
        self._dep_rotate(eng)
        waits = self._dep_waits(eng, deps)
        if self.dcnt[eng] > 0:
            waits.append((self.dsem[eng], self.dcnt[eng]))
        elif self.dep_carry[eng] is not None:
            waits.append(self.dep_carry[eng])
            self.dep_carry[eng] = None
        self.items.append((eng, waits, fn, (self.dsem[eng], inc)))
        self.dcnt[eng] += inc
        return (eng, self.dsem[eng], self.dcnt[eng])

    def chain_d(self, eng, fn_incs, deps):
        if self.serial_mode:
            return self.chain(eng, fn_incs)
        self._dep_rotate(eng)
        waits = self._dep_waits(eng, deps)
        if self.dcnt[eng] > 0:
            waits.append((self.dsem[eng], self.dcnt[eng]))
        elif self.dep_carry[eng] is not None:
            waits.append(self.dep_carry[eng])
            self.dep_carry[eng] = None
        for i, (fn, inc) in enumerate(fn_incs):
            self.items.append((eng, waits if i == 0 else [], fn, (self.dsem[eng], inc)))
            self.dcnt[eng] += inc
        return (eng, self.dsem[eng], self.dcnt[eng])

    def mm_chain_d(self, fns, deps):
        return self.chain_d("tensor", [(f, 0) for f in fns[:-1]] + [(fns[-1], 1)], deps)

    def dma_chain_d(self, fns, deps):
        return self.chain_d("sync", [(f, 16) for f in fns], deps)

    def emit(self):
        final_sem = self.sems[-1]
        final_val = self.total - self.base
        final_carry = list(self.carry)
        with self.nc.Block() as blk:
            for name in ENGINES:
                lst = [(w, f, u) for (e, w, f, u) in self.items if e == name]
                if not lst and name != "sync":
                    continue

                def body(eng, lst=lst, name=name):
                    for (waits, f, (sem, inc)) in lst:
                        for (s, v) in waits:
                            eng.wait_ge(s, v)
                        inst = f(eng)
                        if inc:
                            inst.then_inc(sem, inc)
                    if name == "sync":
                        for (s, v) in final_carry:
                            eng.wait_ge(s, v)
                        for e2 in ENGINES:
                            if self.dcnt[e2] > 0:
                                eng.wait_ge(self.dsem[e2], self.dcnt[e2])
                        eng.wait_ge(final_sem, final_val)
                getattr(blk, name)(body)


class TView:

    def __init__(self, h, pn, c0, cn):
        self.h = h
        self.pn = pn
        self.c0 = c0
        self.cn = cn

    def _ps(self, s):
        start = 0 if s.start is None else s.start
        stop = self.pn if s.stop is None else s.stop
        return slice(start, stop)

    def _cs(self, s):
        start = 0 if s.start is None else s.start
        stop = self.cn if s.stop is None else s.stop
        return slice(self.c0 + start, self.c0 + stop)

    def __getitem__(self, idx):
        if not isinstance(idx, tuple):
            idx = (idx, slice(None))
        return self.h[self._ps(idx[0]), self._cs(idx[1])]


POOL_BYTES = 199 * 1024
DEP_MOE = True
DEP_SFFN = True


class PoolAlloc:

    def __init__(self, bf_h, f32_h):
        self.bf = bf_h
        self.f32 = f32_h
        self.off = 0

    def t(self, shape, dtype):
        assert len(shape) == 2
        esz = 2 if dtype == BF16 else 4
        per_b = shape[1] * esz
        off = self.off
        self.off = (off + per_b + 31) & ~31
        assert self.off <= POOL_BYTES, f"pool overflow: {self.off}"
        h = self.bf if dtype == BF16 else self.f32
        return TView(h, shape[0], off // esz, shape[1])


def emit_ln(p, x_ap, tile, xt, xcf, sq, xnt, cols):
    mean = cols[:, 0:1]
    var = cols[:, 1:2]
    rstd = cols[:, 2:3]
    p.dma("sync", lambda e: e.dma_start(out=xt[:], in_=x_ap[tile * 128:(tile + 1) * 128, :]))
    p.op("vector", lambda e: e.tensor_reduce(mean, xt[:], AX.X, ALU.add))
    p.op("vector", lambda e: e.tensor_scalar_mul(mean, mean, 1.0 / DM))
    p.op("vector", lambda e: e.tensor_scalar_sub(xcf[:], xt[:], mean))
    p.op("scalar", lambda e: e.activation(out=sq[:, 0:DM], in_=xcf[:], func=AF.Square))
    p.op("vector", lambda e: e.tensor_reduce(var, sq[:, 0:DM], AX.X, ALU.add))
    p.op("vector", lambda e: e.tensor_scalar(var, var, 1.0 / DM, 1e-5, ALU.mult, ALU.add))
    p.op("scalar", lambda e: e.activation(out=var, in_=var, func=AF.Sqrt))
    p.op("vector", lambda e: e.reciprocal(rstd, var))
    p.op("vector", lambda e: e.tensor_scalar_mul(xnt[:], xcf[:], rstd))


def emit_xnf_tile(p, xnt, xnf, ttl, stride):
    p.dma_chain([
        (lambda e, k=k: e.dma_start_transpose(
            out=xnf[:, k * stride + ttl * 128: k * stride + ttl * 128 + 128],
            in_=xnt[:, k * 128:(k + 1) * 128]))
        for k in range(KC)])



def emit_moe(p, st, ps, pa, x_ap, xo_ap, cast_weights):
    E, HID = 8, 1536
    MC = HID // 128
    w1_h, w2_h, w3_h, wr_h = st["W1"], st["W2"], st["W3"], st["Wr"]
    w1b_s, w2b_s, w3b_s, xn_s = st["w1b"], st["w2b"], st["w3b"], st["xn_scr"]
    if True:
        A = pa.t
        acc = A([128, 8 * 1024], F32)
        xnf = A([128, KC * 1024], BF16)
        w1sb = A([128, KC * HID], BF16)
        w3sb = A([128, KC * HID], BF16)
        w2sb = A([128, MC * DM], BF16)
        wrsb = A([128, KC * E], BF16)
        stg = A([128, HID], F32)
        stgb = A([128, HID], BF16)
        xt = A([128, DM], F32)
        xcf = A([128, DM], F32)
        xnt = A([128, DM], BF16)
        comb = A([128, NTILE * E], F32)
        hs = A([128, 2 * 512], BF16)
        gbuf = A([128, 2 * 512], BF16)
        z = A([128, 2 * MC * 512], BF16)
        cols = A([128, 8], F32)
        lg = A([128, E], F32)
        tmp = A([128, E], F32)
        tmp2 = A([128, E], F32)
        mask1 = A([128, E], F32)
        mask2 = A([128, E], F32)
        c1 = A([128, E], F32)
        gc = A([128, 8], F32)
        psH = ps["P1"]
        psG = ps["P2"]
        psO = ps["P3"]
        psL = ps["P5"]

        if cast_weights:
            for ex in range(E):
                for (src, dst, width, nchunk) in ((w1_h, w1b_s, HID, KC), (w3_h, w3b_s, HID, KC), (w2_h, w2b_s, DM, MC)):
                    for k in range(nchunk):
                        p.dma("sync", lambda e, src=src, ex=ex, k=k, width=width: e.dma_start(
                            out=stg[:, 0:width], in_=src[ex, k * 128:(k + 1) * 128, :]))
                        p.op("vector", lambda e, width=width: e.tensor_copy(stgb[:, 0:width], stg[:, 0:width]))
                        p.dma("sync", lambda e, dst=dst, ex=ex, k=k, width=width: e.dma_start(
                            out=dst[ex, k * 128:(k + 1) * 128, :], in_=stgb[:, 0:width]))
        for k in range(KC):
            p.dma("sync", lambda e, k=k: e.dma_start(out=stg[:, 0:E], in_=wr_h[k * 128:(k + 1) * 128, :]))
            p.op("vector", lambda e, k=k: e.tensor_copy(wrsb[:, k * E:(k + 1) * E], stg[:, 0:E]))

        for ps4 in range(4):
            for ttl in range(8):
                tile = ps4 * 8 + ttl
                emit_ln(p, x_ap, tile, xt, xcf, stg, xnt, cols)
                emit_xnf_tile(p, xnt, xnf, ttl, 1024)
                p.mm_chain([
                    (lambda e, k=k, ttl=ttl: e.matmul(
                        psL[:, 0:E],
                        xnf[:, k * 1024 + ttl * 128: k * 1024 + ttl * 128 + 128],
                        wrsb[:, k * E:(k + 1) * E],
                        start=(k == 0), stop=(k == KC - 1)))
                    for k in range(KC)])
                m1 = gc[:, 0:1]; m2 = gc[:, 1:2]; e2 = gc[:, 2:3]
                dd = gc[:, 3:4]; w1h = gc[:, 4:5]; w2h = gc[:, 5:6]
                p.op("vector", lambda e: e.tensor_copy(lg[:], psL[:, 0:E]))
                p.op("vector", lambda e: e.tensor_reduce(m1, lg[:], AX.X, ALU.max))
                p.op("vector", lambda e: e.tensor_scalar_sub(tmp[:], lg[:], m1))
                p.op("vector", lambda e: e.tensor_single_scalar(mask1[:], tmp[:], 0.0, ALU.is_equal))
                p.op("vector", lambda e: e.scalar_tensor_tensor(tmp2[:], mask1[:], -1e30, tmp[:], ALU.mult, ALU.add))
                p.op("vector", lambda e: e.tensor_reduce(m2, tmp2[:], AX.X, ALU.max))
                p.op("scalar", lambda e: e.activation(out=e2, in_=m2, func=AF.Exp))
                p.op("vector", lambda e: e.tensor_scalar_add(dd, e2, 1.0))
                p.op("vector", lambda e: e.reciprocal(w1h, dd))
                p.op("vector", lambda e: e.tensor_scalar_mul(w1h, w1h, 0.5))
                p.op("vector", lambda e: e.tensor_scalar(w2h, w1h, -1.0, 0.5, ALU.mult, ALU.add))
                p.op("vector", lambda e: e.tensor_single_scalar(mask2[:], tmp2[:], m2, ALU.is_equal))
                p.op("vector", lambda e: e.tensor_scalar_mul(c1[:], mask1[:], w1h))
                p.op("vector", lambda e: e.tensor_scalar_mul(mask2[:], mask2[:], w2h))
                p.op("vector", lambda e, tile=tile: e.tensor_add(comb[:, tile * E:(tile + 1) * E], c1[:], mask2[:]))
            p.dma_chain([
                (lambda e, k=k, ps4=ps4: e.dma_start(
                    out=xn_s[:, k * T + ps4 * 1024: k * T + ps4 * 1024 + 1024],
                    in_=xnf[:, k * 1024:(k + 1) * 1024]))
                for k in range(KC)])

        psHs = [ps["P1"], ps["P4"]]
        psGs = [ps["P2"], ps["P7"]]
        psOs = [ps["P3"], ps["P8"]]

        def w13_load(ex, deps):
            return p.dma_chain_d(
                [(lambda e, k=k, ex=ex: e.dma_start(
                    out=w1sb[:, k * HID:(k + 1) * HID], in_=w1b_s[ex, k * 128:(k + 1) * 128, :]))
                 for k in range(KC)] +
                [(lambda e, k=k, ex=ex: e.dma_start(
                    out=w3sb[:, k * HID:(k + 1) * HID], in_=w3b_s[ex, k * 128:(k + 1) * 128, :]))
                 for k in range(KC)], deps)

        def w2_load(ex, deps):
            return p.dma_chain_d(
                [(lambda e, k=k, ex=ex: e.dma_start(
                    out=w2sb[:, k * DM:(k + 1) * DM], in_=w2b_s[ex, k * 128:(k + 1) * 128, :]))
                 for k in range(MC)], deps)

        sil_hist = [None, None]
        cpy_hist = [None, None]
        stt_hist = [None, None]
        prev_w2_read = None
        w13_tok = None
        w2_tok = None
        for ps4 in range(4):
            p.op("vector", lambda e: e.memset(acc[:], 0.0))
            xnf_tok = p.dma_chain([
                (lambda e, k=k, ps4=ps4: e.dma_start(
                    out=xnf[:, k * 1024:(k + 1) * 1024],
                    in_=xn_s[:, k * T + ps4 * 1024: k * T + ps4 * 1024 + 1024]))
                for k in range(KC)])
            if ps4 == 0:
                w13_tok = w13_load(0, [])
                w2_tok = w2_load(0, [])
            for ex in range(E):
                mmg_last = None
                tt_last = [None, None]
                for m in range(2 * MC):
                    nt2, ml = m // MC, m % MC
                    tb = nt2 * 512
                    par = m % 2
                    deps_h = [w13_tok, sil_hist[par]]
                    if m == 0:
                        deps_h.append(xnf_tok)
                    mmh = p.mm_chain_d([
                        (lambda e, k=k, ml=ml, tb=tb, par=par: e.matmul(
                            psHs[par][:], w1sb[:, k * HID + ml * 128: k * HID + ml * 128 + 128],
                            xnf[:, k * 1024 + tb: k * 1024 + tb + 512],
                            start=(k == 0), stop=(k == KC - 1)))
                        for k in range(KC)], deps_h)
                    sil = p.op_d("scalar", lambda e, par=par: e.activation(
                        out=hs[:, par * 512:(par + 1) * 512], in_=psHs[par][:], func=AF.Silu), [mmh])
                    mmg = p.mm_chain_d([
                        (lambda e, k=k, ml=ml, tb=tb, par=par: e.matmul(
                            psGs[par][:], w3sb[:, k * HID + ml * 128: k * HID + ml * 128 + 128],
                            xnf[:, k * 1024 + tb: k * 1024 + tb + 512],
                            start=(k == 0), stop=(k == KC - 1)))
                        for k in range(KC)], [cpy_hist[par]])
                    cpy = p.op_d("vector", lambda e, par=par: e.tensor_copy(
                        gbuf[:, par * 512:(par + 1) * 512], psGs[par][:]), [mmg])
                    deps_tt = [sil]
                    if m == 0:
                        deps_tt.append(prev_w2_read)
                    ttok = p.op_d("vector", lambda e, m=m, par=par: e.tensor_mul(
                        z[:, m * 512:(m + 1) * 512],
                        hs[:, par * 512:(par + 1) * 512],
                        gbuf[:, par * 512:(par + 1) * 512]), deps_tt)
                    sil_hist[par] = ttok
                    cpy_hist[par] = ttok
                    tt_last[nt2] = ttok
                    mmg_last = mmg
                nxt = ex + 1 if ex < E - 1 else (0 if ps4 < 3 else None)
                if nxt is not None:
                    w13_next = w13_load(nxt, [mmg_last])
                mmo_last = None
                for it in range(16):
                    nt2, tq, cb = it // 8, (it // 2) % 4, it % 2
                    ttl = nt2 * 4 + tq
                    tile = ps4 * 8 + ttl
                    par = it % 2
                    mmo = p.mm_chain_d([
                        (lambda e, k=k, nt2=nt2, tq=tq, cb=cb, par=par: e.matmul(
                            psOs[par][:], z[:, (nt2 * MC + k) * 512 + tq * 128: (nt2 * MC + k) * 512 + tq * 128 + 128],
                            w2sb[:, k * DM + cb * 512: k * DM + cb * 512 + 512],
                            start=(k == 0), stop=(k == MC - 1)))
                        for k in range(MC)], [tt_last[nt2], w2_tok, stt_hist[par]])
                    stt_hist[par] = p.op_d("vector", lambda e, ttl=ttl, cb=cb, ex=ex, tile=tile, par=par: e.scalar_tensor_tensor(
                        acc[:, ttl * 1024 + cb * 512: ttl * 1024 + cb * 512 + 512],
                        psOs[par][:], comb[:, tile * E + ex: tile * E + ex + 1],
                        acc[:, ttl * 1024 + cb * 512: ttl * 1024 + cb * 512 + 512],
                        ALU.mult, ALU.add), [mmo])
                    mmo_last = mmo
                prev_w2_read = mmo_last
                if nxt is not None:
                    w13_tok = w13_next
                    w2_tok = w2_load(nxt, [mmo_last])
            for ttl in range(8):
                tile = ps4 * 8 + ttl
                p.dma("sync", lambda e, tile=tile: e.dma_start(out=xt[:], in_=x_ap[tile * 128:(tile + 1) * 128, :]))
                p.op("vector", lambda e, ttl=ttl: e.tensor_add(xcf[:], xt[:], acc[:, ttl * 1024:(ttl + 1) * 1024]))
                p.dma("sync", lambda e, tile=tile: e.dma_start(out=xo_ap[tile * 128:(tile + 1) * 128, :], in_=xcf[:]))



def emit_lat(p, st, ps, pa, x_ap, xo_ap):
    k123_h, abc_h, vwT_h, owT_h = st["k123T"], st["abc"], st["vwT"], st["owT"]
    nb_h, sel_h, blk_h = st["nb128"], st["selB"], st["blk16"]
    if True:
        A = pa.t
        xnf = A([128, KC * 512], BF16)
        k123 = A([128, KC * 96], BF16)
        vw = A([128, KC * DM], BF16)
        ow = A([128, KC * DM], BF16)
        nb = A([128, T], F32)
        selB = A([16, KC * 128], F32)
        blk16 = A([128, KC * 16], BF16)
        abc = A([128, 48], F32)
        stg = A([128, DM], F32)
        p1r = A([16, T], F32)
        p2er = A([16, T], F32)
        sinm = A([128, 512], F32)
        scm = A([128, 512], BF16)
        xt = A([128, DM], F32)
        xcf = A([128, DM], F32)
        xnt = A([128, DM], BF16)
        cols = A([128, 8], F32)
        crs = A([16, 512], F32)
        wr_ = A([16, 512], F32)
        dr_ = A([16, 512], F32)
        rden = A([16, 512], F32)
        cumc = A([16, 512], F32)
        ones16 = A([16, 512], F32)
        carr_c = A([16, 1], F32)
        vf = A([128, KC * 512], F32)
        ones128 = A([128, 512], F32)
        ceb = A([128, 512], F32)
        wb = A([128, 512], F32)
        db = A([128, 512], F32)
        rdb = A([128, 512], F32)
        cv = A([128, 512], F32)
        cumv = A([128, 512], F32)
        carr_v = A([128, KC], F32)
        o1 = A([128, 512], F32)
        o2 = A([128, 512], F32)
        obf = A([128, KC * 512], BF16)
        psA = ps["P1"]
        psB = ps["P2"]
        psC = TView(ps["P3"], 96, 0, 512)
        ps16 = TView(ps["P4"], 16, 0, 512)

        p.dma("sync", lambda e: e.dma_start(out=nb[:], in_=nb_h[:]))
        p.dma("sync", lambda e: e.dma_start(out=selB[:], in_=sel_h[:]))
        p.dma("sync", lambda e: e.dma_start(out=blk16[:], in_=blk_h[:]))
        p.dma("sync", lambda e: e.dma_start(out=abc[:], in_=abc_h[:]))
        for k in range(KC):
            p.dma("sync", lambda e, k=k: e.dma_start(out=stg[:, 0:96], in_=k123_h[k * 128:(k + 1) * 128, :]))
            p.op("vector", lambda e, k=k: e.tensor_copy(k123[:, k * 96:(k + 1) * 96], stg[:, 0:96]))
            p.dma("sync", lambda e, k=k: e.dma_start(out=stg[:], in_=vwT_h[k * 128:(k + 1) * 128, :]))
            p.op("vector", lambda e, k=k: e.tensor_copy(vw[:, k * DM:(k + 1) * DM], stg[:]))
            p.dma("sync", lambda e, k=k: e.dma_start(out=stg[:], in_=owT_h[k * 128:(k + 1) * 128, :]))
            p.op("vector", lambda e, k=k: e.tensor_copy(ow[:, k * DM:(k + 1) * DM], stg[:]))
        p.op("vector", lambda e: e.memset(ones16[:], 1.0))
        p.op("vector", lambda e: e.memset(ones128[:], 1.0))
        p.op("vector", lambda e: e.memset(carr_c[:], 0.0))
        p.op("vector", lambda e: e.memset(carr_v[:], 0.0))

        for nt in range(8):
            sl = slice(nt * 512, nt * 512 + 512)
            for which in range(2):
                for hp in range(KC):
                    ai = (0 if which == 0 else 2) * KC + hp
                    bi = (1 if which == 0 else 3) * KC + hp
                    ci = 4 * KC + hp
                    p.op("scalar", lambda e, ai=ai, bi=bi, sl=sl: e.activation(
                        out=sinm[:], in_=nb[:, sl], func=AF.Sin,
                        bias=abc[:, bi:bi + 1], scale=abc[:, ai:ai + 1]))
                    p.op("vector", lambda e, ci=ci: e.tensor_scalar_mul(scm[:], sinm[:], abc[:, ci:ci + 1]))
                    p.op("tensor", lambda e, hp=hp: e.matmul(
                        ps16[:], blk16[:, hp * 16:(hp + 1) * 16], scm[:],
                        start=(hp == 0), stop=(hp == KC - 1)))
                if which == 0:
                    p.op("vector", lambda e, sl=sl: e.tensor_copy(p1r[:, sl], ps16[:]))
                else:
                    p.op("vector", lambda e: e.tensor_scalar(crs[:], ps16[:], 20.0, -20.0, ALU.min, ALU.max))
                    p.op("scalar", lambda e, sl=sl: e.activation(out=p2er[:, sl], in_=crs[:], func=AF.Exp))

        for nt in range(8):
            sl = slice(nt * 512, nt * 512 + 512)
            for tq in range(4):
                tile = nt * 4 + tq
                emit_ln(p, x_ap, tile, xt, xcf, stg, xnt, cols)
                emit_xnf_tile(p, xnt, xnf, tq, 512)
            p.mm_chain([
                (lambda e, k=k: e.matmul(
                    psC[:], k123[:, k * 96:(k + 1) * 96], xnf[:, k * 512:(k + 1) * 512],
                    start=(k == 0), stop=(k == KC - 1)))
                for k in range(KC)])
            p.op("vector", lambda e, sl=sl: e.tensor_add(crs[:], psC[0:16, :], p1r[:, sl]))
            p.op("vector", lambda e: e.tensor_scalar(crs[:], crs[:], 20.0, -20.0, ALU.min, ALU.max))
            p.op("scalar", lambda e: e.activation(out=crs[:], in_=crs[:], func=AF.Exp))
            p.op("vector", lambda e: e.tensor_scalar(dr_[:], psC[32:48, :], 20.0, -20.0, ALU.min, ALU.max))
            p.op("scalar", lambda e: e.activation(out=dr_[:], in_=dr_[:], func=AF.Exp))
            p.op("vector", lambda e: e.tensor_scalar(wr_[:], psC[64:80, :], 20.0, -20.0, ALU.min, ALU.max))
            p.op("scalar", lambda e: e.activation(out=wr_[:], in_=wr_[:], func=AF.Exp))
            p.op("vector", lambda e, sl=sl: e.tensor_mul(wr_[:], wr_[:], p2er[:, sl]))
            p.op("vector", lambda e: e.tensor_tensor_scan(cumc[:], ones16[:], crs[:], carr_c[:, 0:1], ALU.mult, ALU.add))
            p.op("vector", lambda e: e.tensor_copy(carr_c[:], cumc[:, 511:512]))
            p.op("vector", lambda e: e.tensor_mul(rden[:], cumc[:], wr_[:]))
            p.op("vector", lambda e: e.tensor_add(rden[:], rden[:], dr_[:]))
            p.op("vector", lambda e: e.reciprocal(rden[:], rden[:]))
            for m in range(KC):
                p.mm_chain([
                    (lambda e, k=k, m=m: e.matmul(
                        psA[:], vw[:, k * DM + m * 128: k * DM + m * 128 + 128],
                        xnf[:, k * 512:(k + 1) * 512],
                        start=(k == 0), stop=(k == KC - 1)))
                    for k in range(KC)])
                p.op("vector", lambda e, m=m: e.tensor_copy(vf[:, m * 512:(m + 1) * 512], psA[:]))
            for hp in range(KC):
                for (src, dst) in ((crs, ceb), (wr_, wb), (dr_, db), (rden, rdb)):
                    p.op("tensor", lambda e, hp=hp, src=src: e.matmul(
                        psB[:], selB[:, hp * 128:(hp + 1) * 128], src[:],
                        start=True, stop=True))
                    p.op("vector", lambda e, dst=dst: e.tensor_copy(dst[:], psB[:]))
                vsl = vf[:, hp * 512:(hp + 1) * 512]
                p.op("vector", lambda e, vsl=vsl: e.tensor_mul(cv[:], ceb[:], vsl))
                p.op("vector", lambda e, hp=hp: e.tensor_tensor_scan(
                    cumv[:], ones128[:], cv[:], carr_v[:, hp:hp + 1], ALU.mult, ALU.add))
                p.op("vector", lambda e, hp=hp: e.tensor_copy(carr_v[:, hp:hp + 1], cumv[:, 511:512]))
                p.op("vector", lambda e: e.tensor_mul(o1[:], cumv[:], wb[:]))
                p.op("vector", lambda e, vsl=vsl: e.tensor_mul(o2[:], vsl, db[:]))
                p.op("vector", lambda e: e.tensor_add(o1[:], o1[:], o2[:]))
                p.op("vector", lambda e, hp=hp: e.tensor_mul(obf[:, hp * 512:(hp + 1) * 512], o1[:], rdb[:]))
            for tq in range(4):
                tile = nt * 4 + tq
                p.dma("sync", lambda e, tile=tile: e.dma_start(out=xt[:], in_=x_ap[tile * 128:(tile + 1) * 128, :]))
                for cb in range(2):
                    p.mm_chain([
                        (lambda e, k=k, tq=tq, cb=cb: e.matmul(
                            psA[:], obf[:, k * 512 + tq * 128: k * 512 + tq * 128 + 128],
                            ow[:, k * DM + cb * 512: k * DM + cb * 512 + 512],
                            start=(k == 0), stop=(k == KC - 1)))
                        for k in range(KC)])
                    p.op("vector", lambda e, cb=cb: e.tensor_add(
                        xcf[:, cb * 512:(cb + 1) * 512], xt[:, cb * 512:(cb + 1) * 512], psA[:]))
                p.dma("sync", lambda e, tile=tile: e.dma_start(out=xo_ap[tile * 128:(tile + 1) * 128, :], in_=xcf[:]))



def emit_sffn(p, st, ps, pa, x_ap, xo_ap):
    FF = 4096
    FC = FF // 128
    wfcT_h, wprT_h = st["WfcT"], st["WprT"]
    if True:
        A = pa.t
        wfc = A([128, KC * FF], BF16)
        wpr = A([128, FC * DM], BF16)
        stg = A([128, DM], F32)
        xnf = A([128, KC * 512], BF16)
        hb = A([128, FC * 512], BF16)
        xt = A([128, DM], F32)
        xcf = A([128, DM], F32)
        xnt = A([128, DM], BF16)
        cols = A([128, 8], F32)
        relu = A([128, 2 * 512], F32)
        yb = A([128, 512], F32)
        mb = A([128, 512], F32)
        sb_ = A([128, 512], F32)
        ab = A([128, 512], F32)
        tb = A([128, 512], F32)
        psA = ps["P1"]
        psB = ps["P2"]

        for k in range(KC):
            for h2 in range(4):
                p.dma("sync", lambda e, k=k, h2=h2: e.dma_start(
                    out=stg[:], in_=wfcT_h[k * 128:(k + 1) * 128, h2 * 1024:(h2 + 1) * 1024]))
                p.op("vector", lambda e, k=k, h2=h2: e.tensor_copy(
                    wfc[:, k * FF + h2 * 1024: k * FF + (h2 + 1) * 1024], stg[:]))
        for k in range(FC):
            p.dma("sync", lambda e, k=k: e.dma_start(out=stg[:, 0:DM], in_=wprT_h[k * 128:(k + 1) * 128, :]))
            p.op("vector", lambda e, k=k: e.tensor_copy(wpr[:, k * DM:(k + 1) * DM], stg[:, 0:DM]))

        xn_s = st["xn_scr"]
        for nt in range(8):
            for tq in range(4):
                tile = nt * 4 + tq
                emit_ln(p, x_ap, tile, xt, xcf, stg, xnt, cols)
                emit_xnf_tile(p, xnt, xnf, tq, 512)
            p.dma_chain([
                (lambda e, k=k, nt=nt: e.dma_start(
                    out=xn_s[:, k * T + nt * 512: k * T + nt * 512 + 512],
                    in_=xnf[:, k * 512:(k + 1) * 512]))
                for k in range(KC)])

        psAs = [ps["P1"], ps["P4"]]
        psBs = [ps["P2"], ps["P7"]]
        relu_hist = [None, None]
        sq_hist = [None, None]
        copy_hist = [None, None]
        stt_prev = None
        store_prev = None
        mmf_last = None
        mmy_last = None
        for nt in range(8):
            lfns = [(lambda e, k=k, nt=nt: e.dma_start(
                out=xnf[:, k * 512:(k + 1) * 512],
                in_=xn_s[:, k * T + nt * 512: k * T + nt * 512 + 512]))
                for k in range(KC)]
            if nt == 0:
                xnf_tok = p.dma_chain(lfns)
            else:
                xnf_tok = p.dma_chain_d(lfns, [mmf_last])
            sq = None
            for f in range(FC):
                par = f % 2
                mmf = p.mm_chain_d([
                    (lambda e, k=k, f=f, par=par: e.matmul(
                        psAs[par][:], wfc[:, k * FF + f * 128: k * FF + f * 128 + 128],
                        xnf[:, k * 512:(k + 1) * 512],
                        start=(k == 0), stop=(k == KC - 1)))
                    for k in range(KC)], [xnf_tok, relu_hist[par]])
                rel = p.op_d("vector", lambda e, par=par: e.tensor_scalar_max(
                    relu[:, par * 512:(par + 1) * 512], psAs[par][:], 0.0),
                    [mmf, sq_hist[par]])
                sqdeps = [rel]
                if f == 0:
                    sqdeps.append(mmy_last)
                sq = p.op_d("scalar", lambda e, f=f, par=par: e.activation(
                    out=hb[:, f * 512:(f + 1) * 512], in_=relu[:, par * 512:(par + 1) * 512],
                    func=AF.Square), sqdeps)
                relu_hist[par] = rel
                sq_hist[par] = sq
                mmf_last = mmf
            for tq in range(4):
                tile = nt * 4 + tq
                xt_tok = p.op_d("sync", lambda e, tile=tile: e.dma_start(
                    out=xt[:], in_=x_ap[tile * 128:(tile + 1) * 128, :]), [stt_prev], inc=16)
                stt_cb = None
                for cb in range(2):
                    parB = cb
                    mmy = p.mm_chain_d([
                        (lambda e, k=k, tq=tq, cb=cb, parB=parB: e.matmul(
                            psBs[parB][:], hb[:, k * 512 + tq * 128: k * 512 + tq * 128 + 128],
                            wpr[:, k * DM + cb * 512: k * DM + cb * 512 + 512],
                            start=(k == 0), stop=(k == FC - 1)))
                        for k in range(FC)], [sq, copy_hist[parB]])
                    cpy = p.op_d("vector", lambda e, parB=parB: e.tensor_copy(yb[:], psBs[parB][:]), [mmy])
                    copy_hist[parB] = cpy
                    p.op_d("vector", lambda e: e.tensor_single_scalar(sb_[:], yb[:], 1.0, ALU.is_gt), [])
                    p.op_d("vector", lambda e: e.tensor_copy(ab[:], sb_[:]), [])
                    p.op_d("vector", lambda e: e.tensor_copy(mb[:], yb[:]), [])
                    for step in range(4):
                        p.op_d("vector", lambda e: e.scalar_tensor_tensor(tb[:], mb[:], 0.9, yb[:], ALU.mult, ALU.add), [])
                        p.op_d("vector", lambda e: e.tensor_sub(mb[:], tb[:], sb_[:]), [])
                        p.op_d("vector", lambda e: e.tensor_single_scalar(sb_[:], mb[:], 1.0, ALU.is_gt), [])
                        p.op_d("vector", lambda e: e.tensor_add(ab[:], ab[:], sb_[:]), [])
                    stt_cb = p.op_d("vector", lambda e, cb=cb: e.scalar_tensor_tensor(
                        xcf[:, cb * 512:(cb + 1) * 512], ab[:], 0.1,
                        xt[:, cb * 512:(cb + 1) * 512], ALU.mult, ALU.add),
                        [xt_tok, store_prev])
                mmy_last = mmy
                stt_prev = stt_cb
                store_prev = p.op_d("sync", lambda e, tile=tile: e.dma_start(
                    out=xo_ap[tile * 128:(tile + 1) * 128, :], in_=xcf[:]), [stt_cb], inc=16)



def emit_watt(p, st, ps, pa, x_ap, xo_ap):
    NW = 63
    wq_h, wk_h, wv_h, wo_h = st["Wq"], st["Wk"], st["Wv"], st["Wo"]
    bq_h, bk_h, bv_h, bo_h = st["bq"], st["bk"], st["bvr"], st["bor"]
    cnt_h, id_h = st["cnt"], st["idm"]
    q_s, k_s, xn_s = st["q_scr"], st["k_scr"], st["xn_scr"]
    if True:
        A = pa.t
        wsb = A([128, KC * DM], BF16)
        wo2 = A([128, KC * DM], BF16)
        stg = A([128, DM], F32)
        xnf = A([128, KC * 512], BF16)
        xt = A([128, 2 * DM], F32)
        xcf = A([128, 2 * DM], F32)
        lxt = A([128, DM], F32)
        lxcf = A([128, DM], F32)
        xnt = A([128, DM], BF16)
        qev = A([128, 2 * 512], BF16)
        cols = A([128, 8], F32)
        bcol = A([128, KC], F32)
        brow = A([1, 2 * DM], BF16)
        ones1 = A([1, 128], BF16)
        cnts = A([128, 3], F32)
        idm = A([128, 128], BF16)
        qw = A([128, 2 * KC * 128], BF16)
        kw = A([128, 2 * KC * 128], BF16)
        xw = A([128, 2 * KC * 128], BF16)
        vsb = A([128, 2 * DM], BF16)
        pbf = A([128, 2 * 128], BF16)
        ptb = A([128, 2 * 128], BF16)
        mcol = A([128, 4], F32)
        rinv = A([128, 2], F32)
        rrot = A([128, 2], F32)
        acc = A([128, 2 * DM], F32)
        anb = A([128, 2 * DM], BF16)
        afT = A([128, 2 * KC * 128], BF16)
        psBs = [ps["P2"], ps["P7"]]
        psAw = [ps["P1"], ps["P3"]]
        psS = ps["P5"]
        psT = ps["P6"]

        p.dma("sync", lambda e: e.dma_start(out=idm[:], in_=id_h[:]))
        p.dma("sync", lambda e: e.dma_start(out=cnts[:], in_=cnt_h[:]))
        p.op("vector", lambda e: e.memset(ones1[:], 1.0))
        p.dma("sync", lambda e: e.dma_start(out=stg[0:1, 0:DM], in_=bv_h[:]))
        p.op("vector", lambda e: e.tensor_copy(brow[:, 0:DM], stg[0:1, 0:DM]))
        p.dma("sync", lambda e: e.dma_start(out=stg[0:1, 0:DM], in_=bo_h[:]))
        p.op("vector", lambda e: e.tensor_copy(brow[:, DM:2 * DM], stg[0:1, 0:DM]))

        def load_w(w_h, dst):
            for k in range(KC):
                p.dma("sync", lambda e, k=k: e.dma_start(out=stg[:], in_=w_h[k * 128:(k + 1) * 128, :]))
                p.op("vector", lambda e, k=k: e.tensor_copy(dst[:, k * DM:(k + 1) * DM], stg[:]))

        def qk_phase(scr, with_ln):
            act_hist = [None, None]
            store_hist = [None, None]
            for nt in range(8):
                if with_ln:
                    for tq in range(4):
                        tile = nt * 4 + tq
                        emit_ln(p, x_ap, tile, lxt, lxcf, stg, xnt, cols)
                        emit_xnf_tile(p, xnt, xnf, tq, 512)
                    xnf_tok = p.dma_chain([
                        (lambda e, k=k, nt=nt: e.dma_start(
                            out=xn_s[:, k * T + nt * 512: k * T + nt * 512 + 512],
                            in_=xnf[:, k * 512:(k + 1) * 512]))
                        for k in range(KC)])
                else:
                    xnf_tok = p.dma_chain([
                        (lambda e, k=k, nt=nt: e.dma_start(
                            out=xnf[:, k * 512:(k + 1) * 512],
                            in_=xn_s[:, k * T + nt * 512: k * T + nt * 512 + 512]))
                        for k in range(KC)])
                for m in range(KC):
                    pq = m % 2
                    deps = [act_hist[pq]]
                    if m == 0:
                        deps.append(xnf_tok)
                    qmm = p.mm_chain_d([
                        (lambda e, k=k, m=m, pq=pq: e.matmul(
                            psAw[pq][:], wsb[:, k * DM + m * 128: k * DM + m * 128 + 128],
                            xnf[:, k * 512:(k + 1) * 512],
                            start=(k == 0), stop=(k == KC - 1)))
                        for k in range(KC)], deps)
                    act = p.op_d("scalar", lambda e, m=m, pq=pq: e.activation(
                        out=qev[:, pq * 512:(pq + 1) * 512], in_=psAw[pq][:],
                        func=AF.Identity, bias=bcol[:, m:m + 1]),
                        [qmm, store_hist[pq]])
                    store_hist[pq] = p.op_d("sync", lambda e, m=m, nt=nt, pq=pq: e.dma_start(
                        out=scr[:, m * T + nt * 512: m * T + nt * 512 + 512],
                        in_=qev[:, pq * 512:(pq + 1) * 512]), [act], inc=16)
                    act_hist[pq] = act

        load_w(wq_h, wsb)
        p.dma("sync", lambda e: e.dma_start(out=bcol[:], in_=bq_h[:]))
        qk_phase(q_s, with_ln=True)
        load_w(wk_h, wsb)
        p.dma("sync", lambda e: e.dma_start(out=bcol[:], in_=bk_h[:]))
        qk_phase(k_s, with_ln=False)

        load_w(wv_h, wsb)
        load_w(wo_h, wo2)
        p.op("vector", lambda e: e.memset(acc[:], 0.0))

        smm_hist = [None, None]
        omm_hist = [None, None]
        vcopy_hist = [None, None]
        exp_hist = None
        trans_hist = [None, None]
        ptcopy_hist = None
        sttr_hist = [None, None]
        anb_hist = [None, None]
        afr_hist = [None, None]
        add_hist = [None, None]
        store_hist = [None, None]
        ofmm_hist = [None, None]
        fin_state = {}

        def finalize(j, par):
            pj = j % 2
            a_sl = acc[:, par * DM:(par + 1) * DM]
            ci = 0 if j == 0 else (2 if j == NTILE - 1 else 1)
            anb_mul = p.op_d("vector", lambda e, a_sl=a_sl, ci=ci, pj=pj: e.tensor_scalar_mul(
                anb[:, pj * DM:(pj + 1) * DM], a_sl, cnts[:, ci:ci + 1]), [afr_hist[pj]])
            aft = p.dma_chain_d([
                (lambda e, k=k, pj=pj: e.dma_start_transpose(
                    out=afT[:, pj * KC * 128 + k * 128: pj * KC * 128 + (k + 1) * 128],
                    in_=anb[:, pj * DM + k * 128: pj * DM + (k + 1) * 128]))
                for k in range(KC)], [anb_mul, ofmm_hist[pj]])
            xt_tok = p.op_d("sync", lambda e, j=j, pj=pj: e.dma_start(
                out=xt[:, pj * DM:(pj + 1) * DM], in_=x_ap[j * 128:(j + 1) * 128, :]),
                [add_hist[pj]], inc=16)
            addv = None
            last_ofmm = None
            for cb in range(2):
                ofmm = p.mm_chain_d(
                    [(lambda e, k=k, cb=cb, pj=pj: e.matmul(
                        psAw[cb][:], afT[:, pj * KC * 128 + k * 128: pj * KC * 128 + (k + 1) * 128],
                        wo2[:, k * DM + cb * 512: k * DM + cb * 512 + 512],
                        start=(k == 0), stop=False))
                     for k in range(KC)] +
                    [(lambda e, cb=cb: e.matmul(
                        psAw[cb][:], ones1[:], brow[0:1, DM + cb * 512: DM + cb * 512 + 512],
                        start=False, stop=True))], [aft, fin_state.get(("addh", cb))])
                addv = p.op_d("vector", lambda e, cb=cb, pj=pj: e.tensor_add(
                    xcf[:, pj * DM + cb * 512: pj * DM + cb * 512 + 512],
                    xt[:, pj * DM + cb * 512: pj * DM + cb * 512 + 512], psAw[cb][:]),
                    [ofmm, xt_tok, store_hist[pj]])
                fin_state[("addh", cb)] = addv
                last_ofmm = ofmm
            store_hist[pj] = p.op_d("sync", lambda e, j=j, pj=pj: e.dma_start(
                out=xo_ap[j * 128:(j + 1) * 128, :], in_=xcf[:, pj * DM:(pj + 1) * DM]),
                [addv], inc=16)
            clr = p.op_d("vector", lambda e, a_sl=a_sl: e.memset(a_sl, 0.0), [])
            afr_hist[pj] = clr
            add_hist[pj] = addv
            ofmm_hist[pj] = last_ofmm

        for w in range(NW):
            start = 64 * w
            pw = w % 2
            qo = pw * KC * 128
            ld = p.dma_chain_d(
                [(lambda e, k=k, start=start, qo=qo: e.dma_start(
                    out=qw[:, qo + k * 128: qo + (k + 1) * 128], in_=q_s[:, k * T + start: k * T + start + 128]))
                 for k in range(KC)] +
                [(lambda e, k=k, start=start, qo=qo: e.dma_start(
                    out=kw[:, qo + k * 128: qo + (k + 1) * 128], in_=k_s[:, k * T + start: k * T + start + 128]))
                 for k in range(KC)] +
                [(lambda e, k=k, start=start, qo=qo: e.dma_start(
                    out=xw[:, qo + k * 128: qo + (k + 1) * 128], in_=xn_s[:, k * T + start: k * T + start + 128]))
                 for k in range(KC)], [smm_hist[pw]])
            for cb in range(2):
                vmm = p.mm_chain_d(
                    [(lambda e, k=k, cb=cb, qo=qo: e.matmul(
                        psBs[cb][:], xw[:, qo + k * 128: qo + (k + 1) * 128],
                        wsb[:, k * DM + cb * 512: k * DM + cb * 512 + 512],
                        start=(k == 0), stop=False))
                     for k in range(KC)] +
                    [(lambda e, cb=cb: e.matmul(
                        psBs[cb][:], ones1[:], brow[0:1, cb * 512: cb * 512 + 512],
                        start=False, stop=True))], [ld, vcopy_hist[cb], sttr_hist[cb]])
                vcopy_hist[cb] = p.op_d("vector", lambda e, cb=cb, pw=pw: e.tensor_copy(
                    vsb[:, pw * DM + cb * 512: pw * DM + cb * 512 + 512], psBs[cb][:]),
                    [vmm, omm_hist[pw]])
            smm = p.mm_chain_d([
                (lambda e, k=k, qo=qo: e.matmul(
                    psS[:], qw[:, qo + k * 128: qo + (k + 1) * 128], kw[:, qo + k * 128: qo + (k + 1) * 128],
                    start=(k == 0), stop=(k == KC - 1)))
                for k in range(KC)], [ld, exp_hist])
            rmax = p.op_d("vector", lambda e, pw=pw: e.tensor_reduce(
                mcol[:, pw * 2: pw * 2 + 1], psS[:], AX.X, ALU.max), [smm])
            mscale = p.op_d("vector", lambda e, pw=pw: e.tensor_scalar_mul(
                mcol[:, pw * 2 + 1: pw * 2 + 2], mcol[:, pw * 2: pw * 2 + 1], -0.125), [])
            expv = p.op_d("scalar", lambda e, pw=pw: e.activation(
                out=pbf[:, pw * 128:(pw + 1) * 128], in_=psS[:], func=AF.Exp,
                bias=mcol[:, pw * 2 + 1: pw * 2 + 2], scale=0.125),
                [smm, mscale, trans_hist[pw]])
            p.op_d("vector", lambda e, pw=pw: e.tensor_reduce(
                mcol[:, pw * 2: pw * 2 + 1], pbf[:, pw * 128:(pw + 1) * 128], AX.X, ALU.add), [expv])
            rinvv = p.op_d("vector", lambda e, pw=pw: e.reciprocal(
                rinv[:, pw:pw + 1], mcol[:, pw * 2: pw * 2 + 1]), [])
            trans = p.op_d("tensor", lambda e, pw=pw: e.transpose(
                psT[:], pbf[:, pw * 128:(pw + 1) * 128], idm[:]), [expv, ptcopy_hist])
            ptcopy_hist = p.op_d("vector", lambda e, pw=pw: e.tensor_copy(
                ptb[:, pw * 128:(pw + 1) * 128], psT[:]), [trans, omm_hist[pw]])
            trans_hist[pw] = trans
            exp_hist = expv
            j = w // 2
            par = j % 2
            last_omm = None
            if w % 2 == 0:
                for cb in range(2):
                    omm = p.op_d("tensor", lambda e, cb=cb, pw=pw: e.matmul(
                        psBs[cb][:], ptb[:, pw * 128:(pw + 1) * 128],
                        vsb[:, pw * DM + cb * 512: pw * DM + cb * 512 + 512],
                        start=True, stop=True), [ptcopy_hist, vcopy_hist[cb], sttr_hist[cb]])
                    sttr_hist[cb] = p.op_d("vector", lambda e, cb=cb, par=par, pw=pw: e.scalar_tensor_tensor(
                        acc[:, par * DM + cb * 512: par * DM + cb * 512 + 512],
                        psBs[cb][:], rinv[:, pw:pw + 1],
                        acc[:, par * DM + cb * 512: par * DM + cb * 512 + 512],
                        ALU.mult, ALU.add), [omm, rinvv])
                    last_omm = omm
            else:
                rr1 = p.op_d("sync", lambda e, pw=pw: e.dma_start(
                    out=rrot[64:128, pw:pw + 1], in_=rinv[0:64, pw:pw + 1]), [rinvv], inc=16)
                rr2 = p.op_d("sync", lambda e, pw=pw: e.dma_start(
                    out=rrot[0:64, pw:pw + 1], in_=rinv[64:128, pw:pw + 1]), [rinvv], inc=16)
                for cb in range(2):
                    p.op_d("tensor", lambda e, cb=cb, pw=pw: e.matmul(
                        psBs[cb][0:64, :], ptb[:, pw * 128 + 64: pw * 128 + 128],
                        vsb[:, pw * DM + cb * 512: pw * DM + cb * 512 + 512],
                        start=True, stop=True), [ptcopy_hist, vcopy_hist[cb], sttr_hist[cb]])
                    omm = p.op_d("tensor", lambda e, cb=cb, pw=pw: e.matmul(
                        psBs[cb][64:128, :], ptb[:, pw * 128: pw * 128 + 64],
                        vsb[:, pw * DM + cb * 512: pw * DM + cb * 512 + 512],
                        start=True, stop=True), [])
                    s1 = p.op_d("vector", lambda e, cb=cb, par=par, pw=pw: e.scalar_tensor_tensor(
                        acc[64:128, par * DM + cb * 512: par * DM + cb * 512 + 512],
                        psBs[cb][64:128, :], rrot[64:128, pw:pw + 1],
                        acc[64:128, par * DM + cb * 512: par * DM + cb * 512 + 512],
                        ALU.mult, ALU.add), [omm, rr1, rr2])
                    sttr_hist[cb] = p.op_d("vector", lambda e, cb=cb, par=par, pw=pw: e.scalar_tensor_tensor(
                        acc[0:64, (1 - par) * DM + cb * 512: (1 - par) * DM + cb * 512 + 512],
                        psBs[cb][0:64, :], rrot[0:64, pw:pw + 1],
                        acc[0:64, (1 - par) * DM + cb * 512: (1 - par) * DM + cb * 512 + 512],
                        ALU.mult, ALU.add), [])
                    last_omm = omm
                finalize(j, par)
            omm_hist[pw] = last_omm
            smm_hist[pw] = smm
        finalize(NTILE - 1, (NTILE - 1) % 2)



def build_fused():
    nc = bass.Bass("TRN2", target_bir_lowering=False)
    st = {}
    st["x"] = nc.dram_tensor("x", [T, DM], F32, kind="ExternalInput")
    st["xo"] = nc.dram_tensor("xo", [T, DM], F32, kind="ExternalOutput")
    for i in range(1, 6):
        st[f"xi{i}"] = nc.dram_tensor(f"xi{i}", [T, DM], F32, kind="Internal")
    st["W1"] = nc.dram_tensor("W1", [8, DM, 1536], F32, kind="ExternalInput")
    st["W2"] = nc.dram_tensor("W2", [8, 1536, DM], F32, kind="ExternalInput")
    st["W3"] = nc.dram_tensor("W3", [8, DM, 1536], F32, kind="ExternalInput")
    st["Wr"] = nc.dram_tensor("Wr", [DM, 8], F32, kind="ExternalInput")
    st["w1b"] = nc.dram_tensor("w1b", [8, DM, 1536], BF16, kind="Internal")
    st["w2b"] = nc.dram_tensor("w2b", [8, 1536, DM], BF16, kind="Internal")
    st["w3b"] = nc.dram_tensor("w3b", [8, DM, 1536], BF16, kind="Internal")
    st["xn_scr"] = nc.dram_tensor("xn_scr", [128, KC * T], BF16, kind="Internal")
    st["k123T"] = nc.dram_tensor("k123T", [DM, 96], F32, kind="ExternalInput")
    st["abc"] = nc.dram_tensor("abc", [128, 48], F32, kind="ExternalInput")
    st["vwT"] = nc.dram_tensor("vwT", [DM, DM], F32, kind="ExternalInput")
    st["owT"] = nc.dram_tensor("owT", [DM, DM], F32, kind="ExternalInput")
    st["nb128"] = nc.dram_tensor("nb128", [128, T], F32, kind="ExternalInput")
    st["selB"] = nc.dram_tensor("selB", [16, KC * 128], F32, kind="ExternalInput")
    st["blk16"] = nc.dram_tensor("blk16", [128, KC * 16], BF16, kind="ExternalInput")
    st["WfcT"] = nc.dram_tensor("WfcT", [DM, 4096], F32, kind="ExternalInput")
    st["WprT"] = nc.dram_tensor("WprT", [4096, DM], F32, kind="ExternalInput")
    st["Wq"] = nc.dram_tensor("Wq", [DM, DM], F32, kind="ExternalInput")
    st["Wk"] = nc.dram_tensor("Wk", [DM, DM], F32, kind="ExternalInput")
    st["Wv"] = nc.dram_tensor("Wv", [DM, DM], F32, kind="ExternalInput")
    st["Wo"] = nc.dram_tensor("Wo", [DM, DM], F32, kind="ExternalInput")
    st["bq"] = nc.dram_tensor("bq", [128, KC], F32, kind="ExternalInput")
    st["bk"] = nc.dram_tensor("bk", [128, KC], F32, kind="ExternalInput")
    st["bvr"] = nc.dram_tensor("bvr", [1, DM], F32, kind="ExternalInput")
    st["bor"] = nc.dram_tensor("bor", [1, DM], F32, kind="ExternalInput")
    st["cnt"] = nc.dram_tensor("cnt", [128, 3], F32, kind="ExternalInput")
    st["idm"] = nc.dram_tensor("idm", [128, 128], BF16, kind="ExternalInput")
    st["q_scr"] = nc.dram_tensor("q_scr", [128, KC * T], BF16, kind="Internal")
    st["k_scr"] = nc.dram_tensor("k_scr", [128, KC * T], BF16, kind="Internal")

    pool_bf = nc.alloc_sbuf_tensor("pool", [128, POOL_BYTES // 2], BF16)
    pool_f32 = pool_bf.bitcast(F32)
    ps = {
        "P1": nc.psum_tensor("P1", [128, 512], F32).__enter__(),
        "P2": nc.psum_tensor("P2", [128, 512], F32).__enter__(),
        "P3": nc.psum_tensor("P3", [128, 512], F32).__enter__(),
        "P4": nc.psum_tensor("P4", [128, 512], F32).__enter__(),
        "P5": nc.psum_tensor("P5", [128, 128], F32).__enter__(),
        "P6": nc.psum_tensor("P6", [128, 128], BF16).__enter__(),
        "P7": nc.psum_tensor("P7", [128, 512], F32).__enter__(),
        "P8": nc.psum_tensor("P8", [128, 512], F32).__enter__(),
    }
    p = SeqProg(nc)

    def pa():
        return PoolAlloc(pool_bf, pool_f32)

    p.serial_mode = not DEP_MOE
    emit_moe(p, st, ps, pa(), st["x"], st["xi1"], cast_weights=True)
    p.serial_mode = False
    emit_lat(p, st, ps, pa(), st["xi1"], st["xi2"])
    p.serial_mode = not DEP_MOE
    emit_moe(p, st, ps, pa(), st["xi2"], st["xi3"], cast_weights=False)
    p.serial_mode = not DEP_SFFN
    emit_sffn(p, st, ps, pa(), st["xi3"], st["xi4"])
    p.serial_mode = False
    emit_watt(p, st, ps, pa(), st["xi4"], st["xi5"])
    p.serial_mode = not DEP_SFFN
    emit_sffn(p, st, ps, pa(), st["xi5"], st["xo"])
    p.serial_mode = False
    p.emit()
    return nc



def _host_prep(inputs):
    d = {}
    k123T = np.zeros((DM, 96), np.float32)
    k123T[:, 0:16] = inputs["L0_attn_k1"].T
    k123T[:, 32:48] = inputs["L0_attn_k2"].T
    k123T[:, 64:80] = inputs["L0_attn_k3"].T
    d["k123T"] = k123T
    abc = np.zeros((6, KC, 128), np.float32)
    for nm, idx in (("a1", 0), ("b1", 1), ("a2", 2), ("b2", 3), ("c", 4)):
        abc[idx] = inputs["L0_attn_" + nm][:, :, 0].reshape(KC, 128)
    d["abc"] = np.ascontiguousarray(abc.transpose(2, 0, 1).reshape(128, 48))
    d["vwT"] = np.ascontiguousarray(
        inputs["L0_attn_value_weight"].transpose(1, 0, 2).reshape(DM, DM))
    d["owT"] = np.ascontiguousarray(
        inputs["L0_attn_output_weight"].transpose(0, 2, 1).reshape(DM, DM))
    nlin = np.linspace(0.0, 1.0, T, dtype=np.float32)
    d["nb128"] = np.broadcast_to(nlin[None, :], (128, T)).copy()
    selB = np.zeros((KC, 16, 128), np.float32)
    blk16 = np.zeros((KC, 128, 16), np.float32)
    for hp in range(KC):
        for r in range(2):
            selB[hp, 2 * hp + r, 64 * r:64 * (r + 1)] = 1.0
            blk16[hp, 64 * r:64 * (r + 1), 2 * hp + r] = 1.0
    d["selB"] = np.ascontiguousarray(selB.transpose(1, 0, 2).reshape(16, KC * 128))
    d["blk16"] = np.ascontiguousarray(blk16.transpose(1, 0, 2).reshape(128, KC * 16)).astype(BF)
    d["WfcT"] = np.ascontiguousarray(inputs["L1_ffn_W_fc"].T)
    d["WprT"] = np.ascontiguousarray(inputs["L1_ffn_W_proj"].T)
    d["bq"] = np.ascontiguousarray(inputs["L1_attn_bq"].reshape(KC, 128).T)
    d["bk"] = np.ascontiguousarray(inputs["L1_attn_bk"].reshape(KC, 128).T)
    d["bvr"] = inputs["L1_attn_bv"].reshape(1, DM).copy()
    d["bor"] = inputs["L1_attn_bo"].reshape(1, DM).copy()
    cnt = np.full((3, 128), 0.5, np.float32)
    cnt[0, 0:64] = 1.0
    cnt[2, 64:128] = 1.0
    d["cnt"] = np.ascontiguousarray(cnt.T)
    d["idm"] = np.eye(128).astype(BF)
    return d


_PROGS = {}


def _get_fused():
    if "fused" not in _PROGS:
        _PROGS["fused"] = build_fused()
    return _PROGS["fused"]


def kernel(**inputs):
    hp = _host_prep(inputs)
    B = 8
    x = np.ascontiguousarray(inputs["L0_attn_x"].astype(np.float32))
    extra = {"W1": inputs["L0_ffn_W1"], "W2": inputs["L0_ffn_W2"],
             "W3": inputs["L0_ffn_W3"], "Wr": inputs["L0_ffn_Wr"],
             "k123T": hp["k123T"], "abc": hp["abc"], "vwT": hp["vwT"],
             "owT": hp["owT"], "nb128": hp["nb128"], "selB": hp["selB"],
             "blk16": hp["blk16"],
             "WfcT": hp["WfcT"], "WprT": hp["WprT"],
             "Wq": inputs["L1_attn_Wq"], "Wk": inputs["L1_attn_Wk"],
             "Wv": inputs["L1_attn_Wv"], "Wo": inputs["L1_attn_Wo"],
             "bq": hp["bq"], "bk": hp["bk"], "bvr": hp["bvr"],
             "bor": hp["bor"], "cnt": hp["cnt"], "idm": hp["idm"]}
    in_maps = []
    for b in range(B):
        m = {"x": np.ascontiguousarray(x[b])}
        m.update(extra)
        in_maps.append(m)
    res = run_bass_kernel_spmd(_get_fused(), in_maps, core_ids=list(range(B)))
    return np.stack([np.asarray(r["xo"]) for r in res.results], axis=0)


# version 05
# speedup vs baseline: 1.0294x; 1.0294x over previous
import contextlib
import numpy as np
import ml_dtypes
import concourse.bass as bass
import concourse.mybir as mybir
from concourse.bass_utils import run_bass_kernel_spmd

F32 = mybir.dt.float32
BF16 = mybir.dt.bfloat16
AF = mybir.ActivationFunctionType
ALU = mybir.AluOpType
AX = mybir.AxisListType
BF = ml_dtypes.bfloat16

ENGINES = ["sync", "tensor", "vector", "scalar", "gpsimd"]

DM = 1024
T = 4096
KC = DM // 128
NTILE = T // 128


class SeqProg:

    SEM_LIMIT = 20000

    def __init__(self, nc):
        self.nc = nc
        self.sems = [nc.alloc_semaphore("seq0")]
        self.base = 0
        self.total = 0
        self.carry = []
        self.items = []
        self.dsem = {e: nc.alloc_semaphore("d_" + e) for e in ENGINES}
        self.dcnt = {e: 0 for e in ENGINES}
        self.dwaited = {e: 0 for e in ENGINES}
        self.dep_carry = {e: None for e in ENGINES}
        self.dgen = {e: 0 for e in ENGINES}

    def _dep_rotate(self, eng):
        if self.dcnt[eng] > 50000:
            self.carry.append((self.dsem[eng], self.dcnt[eng]))
            self.dep_carry[eng] = (self.dsem[eng], self.dcnt[eng])
            self.dgen[eng] += 1
            self.dsem[eng] = self.nc.alloc_semaphore(f"d_{eng}{self.dgen[eng]}")
            self.dcnt[eng] = 0
            self.dwaited[eng] = 0

    def _rotate_if_needed(self):
        if self.total - self.base > self.SEM_LIMIT:
            self.carry.append((self.sems[-1], self.total - self.base))
            self.sems.append(self.nc.alloc_semaphore(f"seq{len(self.sems)}"))
            self.base = self.total

    def _pre(self):
        self._rotate_if_needed()
        waits = list(self.carry)
        self.carry = []
        if self.total > self.base:
            waits.append((self.sems[-1], self.total - self.base))
        for e in ENGINES:
            if self.dcnt[e] > self.dwaited[e]:
                waits.append((self.dsem[e], self.dcnt[e]))
                self.dwaited[e] = self.dcnt[e]
        return waits

    def op(self, eng, fn, inc=1):
        waits = self._pre()
        self.items.append((eng, waits, fn, (self.sems[-1], inc)))
        self.total += inc
        return (eng, self.sems[-1], self.total - self.base)

    def dma(self, eng, fn):
        return self.op(eng, fn, inc=16)

    def chain(self, eng, fn_incs):
        waits = self._pre()
        for i, (fn, inc) in enumerate(fn_incs):
            self.items.append((eng, waits if i == 0 else [], fn, (self.sems[-1], inc)))
            self.total += inc
        return (eng, self.sems[-1], self.total - self.base)

    def mm_chain(self, fns):
        return self.chain("tensor", [(f, 0) for f in fns[:-1]] + [(fns[-1], 1)])

    def dma_chain(self, fns):
        return self.chain("sync", [(f, 16) for f in fns])

    def _dep_waits(self, eng, deps):
        self._rotate_if_needed()
        waits = {}
        for d in deps:
            if d is None:
                continue
            (e2, s2, v2) = d
            if e2 == eng:
                continue
            if s2 in waits:
                waits[s2] = max(waits[s2], v2)
            else:
                waits[s2] = v2
        return [(s, v) for (s, v) in waits.items()]

    serial_mode = False

    def op_d(self, eng, fn, deps, inc=1):
        if self.serial_mode:
            return self.op(eng, fn, inc)
        self._dep_rotate(eng)
        waits = self._dep_waits(eng, deps)
        if self.dcnt[eng] > 0:
            waits.append((self.dsem[eng], self.dcnt[eng]))
        elif self.dep_carry[eng] is not None:
            waits.append(self.dep_carry[eng])
            self.dep_carry[eng] = None
        self.items.append((eng, waits, fn, (self.dsem[eng], inc)))
        self.dcnt[eng] += inc
        return (eng, self.dsem[eng], self.dcnt[eng])

    def chain_d(self, eng, fn_incs, deps):
        if self.serial_mode:
            return self.chain(eng, fn_incs)
        self._dep_rotate(eng)
        waits = self._dep_waits(eng, deps)
        if self.dcnt[eng] > 0:
            waits.append((self.dsem[eng], self.dcnt[eng]))
        elif self.dep_carry[eng] is not None:
            waits.append(self.dep_carry[eng])
            self.dep_carry[eng] = None
        for i, (fn, inc) in enumerate(fn_incs):
            self.items.append((eng, waits if i == 0 else [], fn, (self.dsem[eng], inc)))
            self.dcnt[eng] += inc
        return (eng, self.dsem[eng], self.dcnt[eng])

    def mm_chain_d(self, fns, deps):
        return self.chain_d("tensor", [(f, 0) for f in fns[:-1]] + [(fns[-1], 1)], deps)

    def dma_chain_d(self, fns, deps):
        return self.chain_d("sync", [(f, 16) for f in fns], deps)

    def emit(self):
        final_sem = self.sems[-1]
        final_val = self.total - self.base
        final_carry = list(self.carry)
        with self.nc.Block() as blk:
            for name in ENGINES:
                lst = [(w, f, u) for (e, w, f, u) in self.items if e == name]
                if not lst and name != "sync":
                    continue

                def body(eng, lst=lst, name=name):
                    for (waits, f, (sem, inc)) in lst:
                        for (s, v) in waits:
                            eng.wait_ge(s, v)
                        inst = f(eng)
                        if inc:
                            inst.then_inc(sem, inc)
                    if name == "sync":
                        for (s, v) in final_carry:
                            eng.wait_ge(s, v)
                        for e2 in ENGINES:
                            if self.dcnt[e2] > 0:
                                eng.wait_ge(self.dsem[e2], self.dcnt[e2])
                        eng.wait_ge(final_sem, final_val)
                getattr(blk, name)(body)


class TView:

    def __init__(self, h, pn, c0, cn):
        self.h = h
        self.pn = pn
        self.c0 = c0
        self.cn = cn

    def _ps(self, s):
        start = 0 if s.start is None else s.start
        stop = self.pn if s.stop is None else s.stop
        return slice(start, stop)

    def _cs(self, s):
        start = 0 if s.start is None else s.start
        stop = self.cn if s.stop is None else s.stop
        return slice(self.c0 + start, self.c0 + stop)

    def __getitem__(self, idx):
        if not isinstance(idx, tuple):
            idx = (idx, slice(None))
        return self.h[self._ps(idx[0]), self._cs(idx[1])]


POOL_BYTES = 199 * 1024
DEP_MOE = True
DEP_SFFN = True


class PoolAlloc:

    def __init__(self, bf_h, f32_h):
        self.bf = bf_h
        self.f32 = f32_h
        self.off = 0

    def t(self, shape, dtype):
        assert len(shape) == 2
        esz = 2 if dtype == BF16 else 4
        per_b = shape[1] * esz
        off = self.off
        self.off = (off + per_b + 31) & ~31
        assert self.off <= POOL_BYTES, f"pool overflow: {self.off}"
        h = self.bf if dtype == BF16 else self.f32
        return TView(h, shape[0], off // esz, shape[1])


def emit_ln(p, x_ap, tile, xt, xcf, sq, xnt, cols):
    mean = cols[:, 0:1]
    var = cols[:, 1:2]
    rstd = cols[:, 2:3]
    p.dma("sync", lambda e: e.dma_start(out=xt[:], in_=x_ap[tile * 128:(tile + 1) * 128, :]))
    p.op("vector", lambda e: e.tensor_reduce(mean, xt[:], AX.X, ALU.add))
    p.op("vector", lambda e: e.tensor_scalar_mul(mean, mean, 1.0 / DM))
    p.op("vector", lambda e: e.tensor_scalar_sub(xcf[:], xt[:], mean))
    p.op("scalar", lambda e: e.activation(out=sq[:, 0:DM], in_=xcf[:], func=AF.Square))
    p.op("vector", lambda e: e.tensor_reduce(var, sq[:, 0:DM], AX.X, ALU.add))
    p.op("vector", lambda e: e.tensor_scalar(var, var, 1.0 / DM, 1e-5, ALU.mult, ALU.add))
    p.op("scalar", lambda e: e.activation(out=var, in_=var, func=AF.Sqrt))
    p.op("vector", lambda e: e.reciprocal(rstd, var))
    p.op("vector", lambda e: e.tensor_scalar_mul(xnt[:], xcf[:], rstd))


def emit_xnf_tile(p, xnt, xnf, ttl, stride):
    p.dma_chain([
        (lambda e, k=k: e.dma_start_transpose(
            out=xnf[:, k * stride + ttl * 128: k * stride + ttl * 128 + 128],
            in_=xnt[:, k * 128:(k + 1) * 128]))
        for k in range(KC)])



def emit_moe(p, st, ps, pa, x_ap, xo_ap, cast_weights):
    E, HID = 8, 1536
    MC = HID // 128
    w1_h, w2_h, w3_h, wr_h = st["W1"], st["W2"], st["W3"], st["Wr"]
    w1b_s, w2b_s, w3b_s, xn_s = st["w1b"], st["w2b"], st["w3b"], st["xn_scr"]
    if True:
        A = pa.t
        acc = A([128, 8 * 1024], F32)
        xnf = A([128, KC * 1024], BF16)
        w1sb = A([128, KC * HID], BF16)
        w3sb = A([128, KC * HID], BF16)
        w2sb = A([128, MC * DM], BF16)
        wrsb = A([128, KC * E], BF16)
        stg = A([128, HID], F32)
        stgb = A([128, HID], BF16)
        xt = A([128, DM], F32)
        xcf = A([128, DM], F32)
        xnt = A([128, DM], BF16)
        comb = A([128, NTILE * E], F32)
        hs = A([128, 2 * 512], BF16)
        gbuf = A([128, 2 * 512], BF16)
        z = A([128, 2 * MC * 512], BF16)
        cols = A([128, 8], F32)
        lg = A([128, E], F32)
        tmp = A([128, E], F32)
        tmp2 = A([128, E], F32)
        mask1 = A([128, E], F32)
        mask2 = A([128, E], F32)
        c1 = A([128, E], F32)
        gc = A([128, 8], F32)
        psH = ps["P1"]
        psG = ps["P2"]
        psO = ps["P3"]
        psL = ps["P5"]

        if cast_weights:
            stg2 = A([128, HID], F32)
            stgb2 = A([128, HID], BF16)
            stgs = [stg, stg2]
            stgbs = [stgb, stgb2]
            cp_hist = [None, None]
            stw_hist = [None, None]
            i = 0
            for ex in range(E):
                for (src, dst, width, nchunk) in ((w1_h, w1b_s, HID, KC), (w3_h, w3b_s, HID, KC), (w2_h, w2b_s, DM, MC)):
                    for k in range(nchunk):
                        par = i % 2
                        ldt = p.op_d("sync", lambda e, src=src, ex=ex, k=k, width=width, par=par: e.dma_start(
                            out=stgs[par][:, 0:width], in_=src[ex, k * 128:(k + 1) * 128, :]),
                            [cp_hist[par]], inc=16)
                        cp_hist[par] = p.op_d("vector", lambda e, width=width, par=par: e.tensor_copy(
                            stgbs[par][:, 0:width], stgs[par][:, 0:width]),
                            [ldt, stw_hist[par]])
                        stw_hist[par] = p.op_d("gpsimd", lambda e, dst=dst, ex=ex, k=k, width=width, par=par: e.dma_start(
                            out=dst[ex, k * 128:(k + 1) * 128, :], in_=stgbs[par][:, 0:width]),
                            [cp_hist[par]], inc=16)
                        i += 1
        for k in range(KC):
            p.dma("sync", lambda e, k=k: e.dma_start(out=stg[:, 0:E], in_=wr_h[k * 128:(k + 1) * 128, :]))
            p.op("vector", lambda e, k=k: e.tensor_copy(wrsb[:, k * E:(k + 1) * E], stg[:, 0:E]))

        for ps4 in range(4):
            for ttl in range(8):
                tile = ps4 * 8 + ttl
                emit_ln(p, x_ap, tile, xt, xcf, stg, xnt, cols)
                emit_xnf_tile(p, xnt, xnf, ttl, 1024)
                p.mm_chain([
                    (lambda e, k=k, ttl=ttl: e.matmul(
                        psL[:, 0:E],
                        xnf[:, k * 1024 + ttl * 128: k * 1024 + ttl * 128 + 128],
                        wrsb[:, k * E:(k + 1) * E],
                        start=(k == 0), stop=(k == KC - 1)))
                    for k in range(KC)])
                m1 = gc[:, 0:1]; m2 = gc[:, 1:2]; e2 = gc[:, 2:3]
                dd = gc[:, 3:4]; w1h = gc[:, 4:5]; w2h = gc[:, 5:6]
                p.op("vector", lambda e: e.tensor_copy(lg[:], psL[:, 0:E]))
                p.op("vector", lambda e: e.tensor_reduce(m1, lg[:], AX.X, ALU.max))
                p.op("vector", lambda e: e.tensor_scalar_sub(tmp[:], lg[:], m1))
                p.op("vector", lambda e: e.tensor_single_scalar(mask1[:], tmp[:], 0.0, ALU.is_equal))
                p.op("vector", lambda e: e.scalar_tensor_tensor(tmp2[:], mask1[:], -1e30, tmp[:], ALU.mult, ALU.add))
                p.op("vector", lambda e: e.tensor_reduce(m2, tmp2[:], AX.X, ALU.max))
                p.op("scalar", lambda e: e.activation(out=e2, in_=m2, func=AF.Exp))
                p.op("vector", lambda e: e.tensor_scalar_add(dd, e2, 1.0))
                p.op("vector", lambda e: e.reciprocal(w1h, dd))
                p.op("vector", lambda e: e.tensor_scalar_mul(w1h, w1h, 0.5))
                p.op("vector", lambda e: e.tensor_scalar(w2h, w1h, -1.0, 0.5, ALU.mult, ALU.add))
                p.op("vector", lambda e: e.tensor_single_scalar(mask2[:], tmp2[:], m2, ALU.is_equal))
                p.op("vector", lambda e: e.tensor_scalar_mul(c1[:], mask1[:], w1h))
                p.op("vector", lambda e: e.tensor_scalar_mul(mask2[:], mask2[:], w2h))
                p.op("vector", lambda e, tile=tile: e.tensor_add(comb[:, tile * E:(tile + 1) * E], c1[:], mask2[:]))
            p.dma_chain([
                (lambda e, k=k, ps4=ps4: e.dma_start(
                    out=xn_s[:, k * T + ps4 * 1024: k * T + ps4 * 1024 + 1024],
                    in_=xnf[:, k * 1024:(k + 1) * 1024]))
                for k in range(KC)])

        psHs = [ps["P1"], ps["P4"]]
        psGs = [ps["P2"], ps["P7"]]
        psOs = [ps["P3"], ps["P8"]]

        def w13_load(ex, deps):
            return p.dma_chain_d(
                [(lambda e, k=k, ex=ex: e.dma_start(
                    out=w1sb[:, k * HID:(k + 1) * HID], in_=w1b_s[ex, k * 128:(k + 1) * 128, :]))
                 for k in range(KC)] +
                [(lambda e, k=k, ex=ex: e.dma_start(
                    out=w3sb[:, k * HID:(k + 1) * HID], in_=w3b_s[ex, k * 128:(k + 1) * 128, :]))
                 for k in range(KC)], deps)

        def w2_load(ex, deps):
            return p.dma_chain_d(
                [(lambda e, k=k, ex=ex: e.dma_start(
                    out=w2sb[:, k * DM:(k + 1) * DM], in_=w2b_s[ex, k * 128:(k + 1) * 128, :]))
                 for k in range(MC)], deps)

        sil_hist = [None, None]
        cpy_hist = [None, None]
        stt_hist = [None, None]
        prev_w2_read = None
        w13_tok = None
        w2_tok = None
        for ps4 in range(4):
            p.op("vector", lambda e: e.memset(acc[:], 0.0))
            xnf_tok = p.dma_chain([
                (lambda e, k=k, ps4=ps4: e.dma_start(
                    out=xnf[:, k * 1024:(k + 1) * 1024],
                    in_=xn_s[:, k * T + ps4 * 1024: k * T + ps4 * 1024 + 1024]))
                for k in range(KC)])
            if ps4 == 0:
                w13_tok = w13_load(0, [])
                w2_tok = w2_load(0, [])
            for ex in range(E):
                mmg_last = None
                tt_last = [None, None]
                for m in range(2 * MC):
                    nt2, ml = m // MC, m % MC
                    tb = nt2 * 512
                    par = m % 2
                    deps_h = [w13_tok, sil_hist[par]]
                    if m == 0:
                        deps_h.append(xnf_tok)
                    mmh = p.mm_chain_d([
                        (lambda e, k=k, ml=ml, tb=tb, par=par: e.matmul(
                            psHs[par][:], w1sb[:, k * HID + ml * 128: k * HID + ml * 128 + 128],
                            xnf[:, k * 1024 + tb: k * 1024 + tb + 512],
                            start=(k == 0), stop=(k == KC - 1)))
                        for k in range(KC)], deps_h)
                    sil = p.op_d("scalar", lambda e, par=par: e.activation(
                        out=hs[:, par * 512:(par + 1) * 512], in_=psHs[par][:], func=AF.Silu), [mmh])
                    mmg = p.mm_chain_d([
                        (lambda e, k=k, ml=ml, tb=tb, par=par: e.matmul(
                            psGs[par][:], w3sb[:, k * HID + ml * 128: k * HID + ml * 128 + 128],
                            xnf[:, k * 1024 + tb: k * 1024 + tb + 512],
                            start=(k == 0), stop=(k == KC - 1)))
                        for k in range(KC)], [cpy_hist[par]])
                    cpy = p.op_d("vector", lambda e, par=par: e.tensor_copy(
                        gbuf[:, par * 512:(par + 1) * 512], psGs[par][:]), [mmg])
                    deps_tt = [sil]
                    if m == 0:
                        deps_tt.append(prev_w2_read)
                    ttok = p.op_d("vector", lambda e, m=m, par=par: e.tensor_mul(
                        z[:, m * 512:(m + 1) * 512],
                        hs[:, par * 512:(par + 1) * 512],
                        gbuf[:, par * 512:(par + 1) * 512]), deps_tt)
                    sil_hist[par] = ttok
                    cpy_hist[par] = ttok
                    tt_last[nt2] = ttok
                    mmg_last = mmg
                nxt = ex + 1 if ex < E - 1 else (0 if ps4 < 3 else None)
                if nxt is not None:
                    w13_next = w13_load(nxt, [mmg_last])
                mmo_last = None
                for it in range(16):
                    nt2, tq, cb = it // 8, (it // 2) % 4, it % 2
                    ttl = nt2 * 4 + tq
                    tile = ps4 * 8 + ttl
                    par = it % 2
                    mmo = p.mm_chain_d([
                        (lambda e, k=k, nt2=nt2, tq=tq, cb=cb, par=par: e.matmul(
                            psOs[par][:], z[:, (nt2 * MC + k) * 512 + tq * 128: (nt2 * MC + k) * 512 + tq * 128 + 128],
                            w2sb[:, k * DM + cb * 512: k * DM + cb * 512 + 512],
                            start=(k == 0), stop=(k == MC - 1)))
                        for k in range(MC)], [tt_last[nt2], w2_tok, stt_hist[par]])
                    stt_hist[par] = p.op_d("vector", lambda e, ttl=ttl, cb=cb, ex=ex, tile=tile, par=par: e.scalar_tensor_tensor(
                        acc[:, ttl * 1024 + cb * 512: ttl * 1024 + cb * 512 + 512],
                        psOs[par][:], comb[:, tile * E + ex: tile * E + ex + 1],
                        acc[:, ttl * 1024 + cb * 512: ttl * 1024 + cb * 512 + 512],
                        ALU.mult, ALU.add), [mmo])
                    mmo_last = mmo
                prev_w2_read = mmo_last
                if nxt is not None:
                    w13_tok = w13_next
                    w2_tok = w2_load(nxt, [mmo_last])
            for ttl in range(8):
                tile = ps4 * 8 + ttl
                p.dma("sync", lambda e, tile=tile: e.dma_start(out=xt[:], in_=x_ap[tile * 128:(tile + 1) * 128, :]))
                p.op("vector", lambda e, ttl=ttl: e.tensor_add(xcf[:], xt[:], acc[:, ttl * 1024:(ttl + 1) * 1024]))
                p.dma("sync", lambda e, tile=tile: e.dma_start(out=xo_ap[tile * 128:(tile + 1) * 128, :], in_=xcf[:]))



def emit_lat(p, st, ps, pa, x_ap, xo_ap):
    k123_h, abc_h, vwT_h, owT_h = st["k123T"], st["abc"], st["vwT"], st["owT"]
    nb_h, sel_h, blk_h = st["nb128"], st["selB"], st["blk16"]
    if True:
        A = pa.t
        xnf = A([128, KC * 512], BF16)
        k123 = A([128, KC * 96], BF16)
        vw = A([128, KC * DM], BF16)
        ow = A([128, KC * DM], BF16)
        nb = A([128, T], F32)
        selB = A([16, KC * 128], F32)
        blk16 = A([128, KC * 16], BF16)
        abc = A([128, 48], F32)
        stg = A([128, DM], F32)
        p1r = A([16, T], F32)
        p2er = A([16, T], F32)
        sinm = A([128, 512], F32)
        scm = A([128, 512], BF16)
        xt = A([128, DM], F32)
        xcf = A([128, DM], F32)
        xnt = A([128, DM], BF16)
        cols = A([128, 8], F32)
        crs = A([16, 512], F32)
        wr_ = A([16, 512], F32)
        dr_ = A([16, 512], F32)
        rden = A([16, 512], F32)
        cumc = A([16, 512], F32)
        ones16 = A([16, 512], F32)
        carr_c = A([16, 1], F32)
        vf = A([128, KC * 512], F32)
        ones128 = A([128, 512], F32)
        ceb = A([128, 512], F32)
        wb = A([128, 512], F32)
        db = A([128, 512], F32)
        rdb = A([128, 512], F32)
        cv = A([128, 512], F32)
        cumv = A([128, 512], F32)
        carr_v = A([128, KC], F32)
        o1 = A([128, 512], F32)
        o2 = A([128, 512], F32)
        obf = A([128, KC * 512], BF16)
        psA = ps["P1"]
        psB = ps["P2"]
        psC = TView(ps["P3"], 96, 0, 512)
        ps16 = TView(ps["P4"], 16, 0, 512)

        p.dma("sync", lambda e: e.dma_start(out=nb[:], in_=nb_h[:]))
        p.dma("sync", lambda e: e.dma_start(out=selB[:], in_=sel_h[:]))
        p.dma("sync", lambda e: e.dma_start(out=blk16[:], in_=blk_h[:]))
        p.dma("sync", lambda e: e.dma_start(out=abc[:], in_=abc_h[:]))
        for k in range(KC):
            p.dma("sync", lambda e, k=k: e.dma_start(out=stg[:, 0:96], in_=k123_h[k * 128:(k + 1) * 128, :]))
            p.op("vector", lambda e, k=k: e.tensor_copy(k123[:, k * 96:(k + 1) * 96], stg[:, 0:96]))
            p.dma("sync", lambda e, k=k: e.dma_start(out=stg[:], in_=vwT_h[k * 128:(k + 1) * 128, :]))
            p.op("vector", lambda e, k=k: e.tensor_copy(vw[:, k * DM:(k + 1) * DM], stg[:]))
            p.dma("sync", lambda e, k=k: e.dma_start(out=stg[:], in_=owT_h[k * 128:(k + 1) * 128, :]))
            p.op("vector", lambda e, k=k: e.tensor_copy(ow[:, k * DM:(k + 1) * DM], stg[:]))
        p.op("vector", lambda e: e.memset(ones16[:], 1.0))
        p.op("vector", lambda e: e.memset(ones128[:], 1.0))
        p.op("vector", lambda e: e.memset(carr_c[:], 0.0))
        p.op("vector", lambda e: e.memset(carr_v[:], 0.0))

        for nt in range(8):
            sl = slice(nt * 512, nt * 512 + 512)
            for which in range(2):
                for hp in range(KC):
                    ai = (0 if which == 0 else 2) * KC + hp
                    bi = (1 if which == 0 else 3) * KC + hp
                    ci = 4 * KC + hp
                    p.op("scalar", lambda e, ai=ai, bi=bi, sl=sl: e.activation(
                        out=sinm[:], in_=nb[:, sl], func=AF.Sin,
                        bias=abc[:, bi:bi + 1], scale=abc[:, ai:ai + 1]))
                    p.op("vector", lambda e, ci=ci: e.tensor_scalar_mul(scm[:], sinm[:], abc[:, ci:ci + 1]))
                    p.op("tensor", lambda e, hp=hp: e.matmul(
                        ps16[:], blk16[:, hp * 16:(hp + 1) * 16], scm[:],
                        start=(hp == 0), stop=(hp == KC - 1)))
                if which == 0:
                    p.op("vector", lambda e, sl=sl: e.tensor_copy(p1r[:, sl], ps16[:]))
                else:
                    p.op("vector", lambda e: e.tensor_scalar(crs[:], ps16[:], 20.0, -20.0, ALU.min, ALU.max))
                    p.op("scalar", lambda e, sl=sl: e.activation(out=p2er[:, sl], in_=crs[:], func=AF.Exp))

        for nt in range(8):
            sl = slice(nt * 512, nt * 512 + 512)
            for tq in range(4):
                tile = nt * 4 + tq
                emit_ln(p, x_ap, tile, xt, xcf, stg, xnt, cols)
                emit_xnf_tile(p, xnt, xnf, tq, 512)
            p.mm_chain([
                (lambda e, k=k: e.matmul(
                    psC[:], k123[:, k * 96:(k + 1) * 96], xnf[:, k * 512:(k + 1) * 512],
                    start=(k == 0), stop=(k == KC - 1)))
                for k in range(KC)])
            p.op("vector", lambda e, sl=sl: e.tensor_add(crs[:], psC[0:16, :], p1r[:, sl]))
            p.op("vector", lambda e: e.tensor_scalar(crs[:], crs[:], 20.0, -20.0, ALU.min, ALU.max))
            p.op("scalar", lambda e: e.activation(out=crs[:], in_=crs[:], func=AF.Exp))
            p.op("vector", lambda e: e.tensor_scalar(dr_[:], psC[32:48, :], 20.0, -20.0, ALU.min, ALU.max))
            p.op("scalar", lambda e: e.activation(out=dr_[:], in_=dr_[:], func=AF.Exp))
            p.op("vector", lambda e: e.tensor_scalar(wr_[:], psC[64:80, :], 20.0, -20.0, ALU.min, ALU.max))
            p.op("scalar", lambda e: e.activation(out=wr_[:], in_=wr_[:], func=AF.Exp))
            p.op("vector", lambda e, sl=sl: e.tensor_mul(wr_[:], wr_[:], p2er[:, sl]))
            p.op("vector", lambda e: e.tensor_tensor_scan(cumc[:], ones16[:], crs[:], carr_c[:, 0:1], ALU.mult, ALU.add))
            p.op("vector", lambda e: e.tensor_copy(carr_c[:], cumc[:, 511:512]))
            p.op("vector", lambda e: e.tensor_mul(rden[:], cumc[:], wr_[:]))
            p.op("vector", lambda e: e.tensor_add(rden[:], rden[:], dr_[:]))
            p.op("vector", lambda e: e.reciprocal(rden[:], rden[:]))
            for m in range(KC):
                p.mm_chain([
                    (lambda e, k=k, m=m: e.matmul(
                        psA[:], vw[:, k * DM + m * 128: k * DM + m * 128 + 128],
                        xnf[:, k * 512:(k + 1) * 512],
                        start=(k == 0), stop=(k == KC - 1)))
                    for k in range(KC)])
                p.op("vector", lambda e, m=m: e.tensor_copy(vf[:, m * 512:(m + 1) * 512], psA[:]))
            for hp in range(KC):
                for (src, dst) in ((crs, ceb), (wr_, wb), (dr_, db), (rden, rdb)):
                    p.op("tensor", lambda e, hp=hp, src=src: e.matmul(
                        psB[:], selB[:, hp * 128:(hp + 1) * 128], src[:],
                        start=True, stop=True))
                    p.op("vector", lambda e, dst=dst: e.tensor_copy(dst[:], psB[:]))
                vsl = vf[:, hp * 512:(hp + 1) * 512]
                p.op("vector", lambda e, vsl=vsl: e.tensor_mul(cv[:], ceb[:], vsl))
                p.op("vector", lambda e, hp=hp: e.tensor_tensor_scan(
                    cumv[:], ones128[:], cv[:], carr_v[:, hp:hp + 1], ALU.mult, ALU.add))
                p.op("vector", lambda e, hp=hp: e.tensor_copy(carr_v[:, hp:hp + 1], cumv[:, 511:512]))
                p.op("vector", lambda e: e.tensor_mul(o1[:], cumv[:], wb[:]))
                p.op("vector", lambda e, vsl=vsl: e.tensor_mul(o2[:], vsl, db[:]))
                p.op("vector", lambda e: e.tensor_add(o1[:], o1[:], o2[:]))
                p.op("vector", lambda e, hp=hp: e.tensor_mul(obf[:, hp * 512:(hp + 1) * 512], o1[:], rdb[:]))
            for tq in range(4):
                tile = nt * 4 + tq
                p.dma("sync", lambda e, tile=tile: e.dma_start(out=xt[:], in_=x_ap[tile * 128:(tile + 1) * 128, :]))
                for cb in range(2):
                    p.mm_chain([
                        (lambda e, k=k, tq=tq, cb=cb: e.matmul(
                            psA[:], obf[:, k * 512 + tq * 128: k * 512 + tq * 128 + 128],
                            ow[:, k * DM + cb * 512: k * DM + cb * 512 + 512],
                            start=(k == 0), stop=(k == KC - 1)))
                        for k in range(KC)])
                    p.op("vector", lambda e, cb=cb: e.tensor_add(
                        xcf[:, cb * 512:(cb + 1) * 512], xt[:, cb * 512:(cb + 1) * 512], psA[:]))
                p.dma("sync", lambda e, tile=tile: e.dma_start(out=xo_ap[tile * 128:(tile + 1) * 128, :], in_=xcf[:]))



def emit_sffn(p, st, ps, pa, x_ap, xo_ap):
    FF = 4096
    FC = FF // 128
    wfcT_h, wprT_h = st["WfcT"], st["WprT"]
    if True:
        A = pa.t
        wfc = A([128, KC * FF], BF16)
        wpr = A([128, FC * DM], BF16)
        stg = A([128, DM], F32)
        xnf = A([128, KC * 512], BF16)
        hb = A([128, FC * 512], BF16)
        xt = A([128, DM], F32)
        xcf = A([128, DM], F32)
        xnt = A([128, DM], BF16)
        cols = A([128, 8], F32)
        relu = A([128, 2 * 512], F32)
        yb = A([128, 512], F32)
        mb = A([128, 512], F32)
        sb_ = A([128, 512], F32)
        ab = A([128, 512], F32)
        tb = A([128, 512], F32)
        psA = ps["P1"]
        psB = ps["P2"]

        for k in range(KC):
            for h2 in range(4):
                p.dma("sync", lambda e, k=k, h2=h2: e.dma_start(
                    out=stg[:], in_=wfcT_h[k * 128:(k + 1) * 128, h2 * 1024:(h2 + 1) * 1024]))
                p.op("vector", lambda e, k=k, h2=h2: e.tensor_copy(
                    wfc[:, k * FF + h2 * 1024: k * FF + (h2 + 1) * 1024], stg[:]))
        for k in range(FC):
            p.dma("sync", lambda e, k=k: e.dma_start(out=stg[:, 0:DM], in_=wprT_h[k * 128:(k + 1) * 128, :]))
            p.op("vector", lambda e, k=k: e.tensor_copy(wpr[:, k * DM:(k + 1) * DM], stg[:, 0:DM]))

        xn_s = st["xn_scr"]
        for nt in range(8):
            for tq in range(4):
                tile = nt * 4 + tq
                emit_ln(p, x_ap, tile, xt, xcf, stg, xnt, cols)
                emit_xnf_tile(p, xnt, xnf, tq, 512)
            p.dma_chain([
                (lambda e, k=k, nt=nt: e.dma_start(
                    out=xn_s[:, k * T + nt * 512: k * T + nt * 512 + 512],
                    in_=xnf[:, k * 512:(k + 1) * 512]))
                for k in range(KC)])

        psAs = [ps["P1"], ps["P4"]]
        psBs = [ps["P2"], ps["P7"]]
        relu_hist = [None, None]
        sq_hist = [None, None]
        copy_hist = [None, None]
        stt_prev = None
        store_prev = None
        mmf_last = None
        mmy_last = None
        for nt in range(8):
            lfns = [(lambda e, k=k, nt=nt: e.dma_start(
                out=xnf[:, k * 512:(k + 1) * 512],
                in_=xn_s[:, k * T + nt * 512: k * T + nt * 512 + 512]))
                for k in range(KC)]
            if nt == 0:
                xnf_tok = p.dma_chain(lfns)
            else:
                xnf_tok = p.dma_chain_d(lfns, [mmf_last])
            sq = None
            for f in range(FC):
                par = f % 2
                mmf = p.mm_chain_d([
                    (lambda e, k=k, f=f, par=par: e.matmul(
                        psAs[par][:], wfc[:, k * FF + f * 128: k * FF + f * 128 + 128],
                        xnf[:, k * 512:(k + 1) * 512],
                        start=(k == 0), stop=(k == KC - 1)))
                    for k in range(KC)], [xnf_tok, relu_hist[par]])
                rel = p.op_d("vector", lambda e, par=par: e.tensor_scalar_max(
                    relu[:, par * 512:(par + 1) * 512], psAs[par][:], 0.0),
                    [mmf, sq_hist[par]])
                sqdeps = [rel]
                if f == 0:
                    sqdeps.append(mmy_last)
                sq = p.op_d("scalar", lambda e, f=f, par=par: e.activation(
                    out=hb[:, f * 512:(f + 1) * 512], in_=relu[:, par * 512:(par + 1) * 512],
                    func=AF.Square), sqdeps)
                relu_hist[par] = rel
                sq_hist[par] = sq
                mmf_last = mmf
            for tq in range(4):
                tile = nt * 4 + tq
                xt_tok = p.op_d("sync", lambda e, tile=tile: e.dma_start(
                    out=xt[:], in_=x_ap[tile * 128:(tile + 1) * 128, :]), [stt_prev], inc=16)
                stt_cb = None
                for cb in range(2):
                    parB = cb
                    mmy = p.mm_chain_d([
                        (lambda e, k=k, tq=tq, cb=cb, parB=parB: e.matmul(
                            psBs[parB][:], hb[:, k * 512 + tq * 128: k * 512 + tq * 128 + 128],
                            wpr[:, k * DM + cb * 512: k * DM + cb * 512 + 512],
                            start=(k == 0), stop=(k == FC - 1)))
                        for k in range(FC)], [sq, copy_hist[parB]])
                    cpy = p.op_d("vector", lambda e, parB=parB: e.tensor_copy(yb[:], psBs[parB][:]), [mmy])
                    copy_hist[parB] = cpy
                    p.op_d("vector", lambda e: e.tensor_single_scalar(sb_[:], yb[:], 1.0, ALU.is_gt), [])
                    p.op_d("vector", lambda e: e.tensor_copy(ab[:], sb_[:]), [])
                    p.op_d("vector", lambda e: e.tensor_copy(mb[:], yb[:]), [])
                    for step in range(4):
                        p.op_d("vector", lambda e: e.scalar_tensor_tensor(tb[:], mb[:], 0.9, yb[:], ALU.mult, ALU.add), [])
                        p.op_d("vector", lambda e: e.tensor_sub(mb[:], tb[:], sb_[:]), [])
                        p.op_d("vector", lambda e: e.tensor_single_scalar(sb_[:], mb[:], 1.0, ALU.is_gt), [])
                        p.op_d("vector", lambda e: e.tensor_add(ab[:], ab[:], sb_[:]), [])
                    stt_cb = p.op_d("vector", lambda e, cb=cb: e.scalar_tensor_tensor(
                        xcf[:, cb * 512:(cb + 1) * 512], ab[:], 0.1,
                        xt[:, cb * 512:(cb + 1) * 512], ALU.mult, ALU.add),
                        [xt_tok, store_prev])
                mmy_last = mmy
                stt_prev = stt_cb
                store_prev = p.op_d("sync", lambda e, tile=tile: e.dma_start(
                    out=xo_ap[tile * 128:(tile + 1) * 128, :], in_=xcf[:]), [stt_cb], inc=16)



def emit_watt(p, st, ps, pa, x_ap, xo_ap):
    NW = 63
    wq_h, wk_h, wv_h, wo_h = st["Wq"], st["Wk"], st["Wv"], st["Wo"]
    bq_h, bk_h, bv_h, bo_h = st["bq"], st["bk"], st["bvr"], st["bor"]
    cnt_h, id_h = st["cnt"], st["idm"]
    q_s, k_s, xn_s = st["q_scr"], st["k_scr"], st["xn_scr"]
    if True:
        A = pa.t
        wsb = A([128, KC * DM], BF16)
        wo2 = A([128, KC * DM], BF16)
        stg = A([128, DM], F32)
        xnf = A([128, KC * 512], BF16)
        xt = A([128, 2 * DM], F32)
        xcf = A([128, 2 * DM], F32)
        lxt = A([128, DM], F32)
        lxcf = A([128, DM], F32)
        xnt = A([128, DM], BF16)
        qev = A([128, 2 * 512], BF16)
        cols = A([128, 8], F32)
        bcol = A([128, KC], F32)
        brow = A([1, 2 * DM], BF16)
        ones1 = A([1, 128], BF16)
        cnts = A([128, 3], F32)
        idm = A([128, 128], BF16)
        qw = A([128, 2 * KC * 128], BF16)
        kw = A([128, 2 * KC * 128], BF16)
        xw = A([128, 2 * KC * 128], BF16)
        vsb = A([128, 2 * DM], BF16)
        pbf = A([128, 2 * 128], BF16)
        ptb = A([128, 2 * 128], BF16)
        mcol = A([128, 4], F32)
        rinv = A([128, 2], F32)
        rrot = A([128, 2], F32)
        acc = A([128, 2 * DM], F32)
        anb = A([128, 2 * DM], BF16)
        afT = A([128, 2 * KC * 128], BF16)
        psBs = [ps["P2"], ps["P7"]]
        psAw = [ps["P1"], ps["P3"]]
        psS = ps["P5"]
        psT = ps["P6"]

        p.dma("sync", lambda e: e.dma_start(out=idm[:], in_=id_h[:]))
        p.dma("sync", lambda e: e.dma_start(out=cnts[:], in_=cnt_h[:]))
        p.op("vector", lambda e: e.memset(ones1[:], 1.0))
        p.dma("sync", lambda e: e.dma_start(out=stg[0:1, 0:DM], in_=bv_h[:]))
        p.op("vector", lambda e: e.tensor_copy(brow[:, 0:DM], stg[0:1, 0:DM]))
        p.dma("sync", lambda e: e.dma_start(out=stg[0:1, 0:DM], in_=bo_h[:]))
        p.op("vector", lambda e: e.tensor_copy(brow[:, DM:2 * DM], stg[0:1, 0:DM]))

        def load_w(w_h, dst):
            for k in range(KC):
                p.dma("sync", lambda e, k=k: e.dma_start(out=stg[:], in_=w_h[k * 128:(k + 1) * 128, :]))
                p.op("vector", lambda e, k=k: e.tensor_copy(dst[:, k * DM:(k + 1) * DM], stg[:]))

        def qk_phase(scr, with_ln):
            act_hist = [None, None]
            store_hist = [None, None]
            for nt in range(8):
                if with_ln:
                    for tq in range(4):
                        tile = nt * 4 + tq
                        emit_ln(p, x_ap, tile, lxt, lxcf, stg, xnt, cols)
                        emit_xnf_tile(p, xnt, xnf, tq, 512)
                    xnf_tok = p.dma_chain([
                        (lambda e, k=k, nt=nt: e.dma_start(
                            out=xn_s[:, k * T + nt * 512: k * T + nt * 512 + 512],
                            in_=xnf[:, k * 512:(k + 1) * 512]))
                        for k in range(KC)])
                else:
                    xnf_tok = p.dma_chain([
                        (lambda e, k=k, nt=nt: e.dma_start(
                            out=xnf[:, k * 512:(k + 1) * 512],
                            in_=xn_s[:, k * T + nt * 512: k * T + nt * 512 + 512]))
                        for k in range(KC)])
                for m in range(KC):
                    pq = m % 2
                    deps = [act_hist[pq]]
                    if m == 0:
                        deps.append(xnf_tok)
                    qmm = p.mm_chain_d([
                        (lambda e, k=k, m=m, pq=pq: e.matmul(
                            psAw[pq][:], wsb[:, k * DM + m * 128: k * DM + m * 128 + 128],
                            xnf[:, k * 512:(k + 1) * 512],
                            start=(k == 0), stop=(k == KC - 1)))
                        for k in range(KC)], deps)
                    act = p.op_d("scalar", lambda e, m=m, pq=pq: e.activation(
                        out=qev[:, pq * 512:(pq + 1) * 512], in_=psAw[pq][:],
                        func=AF.Identity, bias=bcol[:, m:m + 1]),
                        [qmm, store_hist[pq]])
                    store_hist[pq] = p.op_d("sync", lambda e, m=m, nt=nt, pq=pq: e.dma_start(
                        out=scr[:, m * T + nt * 512: m * T + nt * 512 + 512],
                        in_=qev[:, pq * 512:(pq + 1) * 512]), [act], inc=16)
                    act_hist[pq] = act

        load_w(wq_h, wsb)
        p.dma("sync", lambda e: e.dma_start(out=bcol[:], in_=bq_h[:]))
        qk_phase(q_s, with_ln=True)
        load_w(wk_h, wsb)
        p.dma("sync", lambda e: e.dma_start(out=bcol[:], in_=bk_h[:]))
        qk_phase(k_s, with_ln=False)

        load_w(wv_h, wsb)
        load_w(wo_h, wo2)
        p.op("vector", lambda e: e.memset(acc[:], 0.0))

        smm_hist = [None, None]
        omm_hist = [None, None]
        vcopy_hist = [None, None]
        exp_hist = None
        trans_hist = [None, None]
        ptcopy_hist = None
        sttr_hist = [None, None]
        anb_hist = [None, None]
        afr_hist = [None, None]
        add_hist = [None, None]
        store_hist = [None, None]
        ofmm_hist = [None, None]
        fin_state = {}

        def finalize(j, par):
            pj = j % 2
            a_sl = acc[:, par * DM:(par + 1) * DM]
            ci = 0 if j == 0 else (2 if j == NTILE - 1 else 1)
            anb_mul = p.op_d("vector", lambda e, a_sl=a_sl, ci=ci, pj=pj: e.tensor_scalar_mul(
                anb[:, pj * DM:(pj + 1) * DM], a_sl, cnts[:, ci:ci + 1]), [afr_hist[pj]])
            aft = p.dma_chain_d([
                (lambda e, k=k, pj=pj: e.dma_start_transpose(
                    out=afT[:, pj * KC * 128 + k * 128: pj * KC * 128 + (k + 1) * 128],
                    in_=anb[:, pj * DM + k * 128: pj * DM + (k + 1) * 128]))
                for k in range(KC)], [anb_mul, ofmm_hist[pj]])
            xt_tok = p.op_d("sync", lambda e, j=j, pj=pj: e.dma_start(
                out=xt[:, pj * DM:(pj + 1) * DM], in_=x_ap[j * 128:(j + 1) * 128, :]),
                [add_hist[pj]], inc=16)
            addv = None
            last_ofmm = None
            for cb in range(2):
                ofmm = p.mm_chain_d(
                    [(lambda e, k=k, cb=cb, pj=pj: e.matmul(
                        psAw[cb][:], afT[:, pj * KC * 128 + k * 128: pj * KC * 128 + (k + 1) * 128],
                        wo2[:, k * DM + cb * 512: k * DM + cb * 512 + 512],
                        start=(k == 0), stop=False))
                     for k in range(KC)] +
                    [(lambda e, cb=cb: e.matmul(
                        psAw[cb][:], ones1[:], brow[0:1, DM + cb * 512: DM + cb * 512 + 512],
                        start=False, stop=True))], [aft, fin_state.get(("addh", cb))])
                addv = p.op_d("vector", lambda e, cb=cb, pj=pj: e.tensor_add(
                    xcf[:, pj * DM + cb * 512: pj * DM + cb * 512 + 512],
                    xt[:, pj * DM + cb * 512: pj * DM + cb * 512 + 512], psAw[cb][:]),
                    [ofmm, xt_tok, store_hist[pj]])
                fin_state[("addh", cb)] = addv
                last_ofmm = ofmm
            store_hist[pj] = p.op_d("sync", lambda e, j=j, pj=pj: e.dma_start(
                out=xo_ap[j * 128:(j + 1) * 128, :], in_=xcf[:, pj * DM:(pj + 1) * DM]),
                [addv], inc=16)
            clr = p.op_d("vector", lambda e, a_sl=a_sl: e.memset(a_sl, 0.0), [])
            afr_hist[pj] = clr
            add_hist[pj] = addv
            ofmm_hist[pj] = last_ofmm

        for w in range(NW):
            start = 64 * w
            pw = w % 2
            qo = pw * KC * 128
            ld = p.dma_chain_d(
                [(lambda e, k=k, start=start, qo=qo: e.dma_start(
                    out=qw[:, qo + k * 128: qo + (k + 1) * 128], in_=q_s[:, k * T + start: k * T + start + 128]))
                 for k in range(KC)] +
                [(lambda e, k=k, start=start, qo=qo: e.dma_start(
                    out=kw[:, qo + k * 128: qo + (k + 1) * 128], in_=k_s[:, k * T + start: k * T + start + 128]))
                 for k in range(KC)] +
                [(lambda e, k=k, start=start, qo=qo: e.dma_start(
                    out=xw[:, qo + k * 128: qo + (k + 1) * 128], in_=xn_s[:, k * T + start: k * T + start + 128]))
                 for k in range(KC)], [smm_hist[pw]])
            for cb in range(2):
                vmm = p.mm_chain_d(
                    [(lambda e, k=k, cb=cb, qo=qo: e.matmul(
                        psBs[cb][:], xw[:, qo + k * 128: qo + (k + 1) * 128],
                        wsb[:, k * DM + cb * 512: k * DM + cb * 512 + 512],
                        start=(k == 0), stop=False))
                     for k in range(KC)] +
                    [(lambda e, cb=cb: e.matmul(
                        psBs[cb][:], ones1[:], brow[0:1, cb * 512: cb * 512 + 512],
                        start=False, stop=True))], [ld, vcopy_hist[cb], sttr_hist[cb]])
                vcopy_hist[cb] = p.op_d("vector", lambda e, cb=cb, pw=pw: e.tensor_copy(
                    vsb[:, pw * DM + cb * 512: pw * DM + cb * 512 + 512], psBs[cb][:]),
                    [vmm, omm_hist[pw]])
            smm = p.mm_chain_d([
                (lambda e, k=k, qo=qo: e.matmul(
                    psS[:], qw[:, qo + k * 128: qo + (k + 1) * 128], kw[:, qo + k * 128: qo + (k + 1) * 128],
                    start=(k == 0), stop=(k == KC - 1)))
                for k in range(KC)], [ld, exp_hist])
            rmax = p.op_d("vector", lambda e, pw=pw: e.tensor_reduce(
                mcol[:, pw * 2: pw * 2 + 1], psS[:], AX.X, ALU.max), [smm])
            mscale = p.op_d("vector", lambda e, pw=pw: e.tensor_scalar_mul(
                mcol[:, pw * 2 + 1: pw * 2 + 2], mcol[:, pw * 2: pw * 2 + 1], -0.125), [])
            expv = p.op_d("scalar", lambda e, pw=pw: e.activation(
                out=pbf[:, pw * 128:(pw + 1) * 128], in_=psS[:], func=AF.Exp,
                bias=mcol[:, pw * 2 + 1: pw * 2 + 2], scale=0.125),
                [smm, mscale, trans_hist[pw]])
            p.op_d("vector", lambda e, pw=pw: e.tensor_reduce(
                mcol[:, pw * 2: pw * 2 + 1], pbf[:, pw * 128:(pw + 1) * 128], AX.X, ALU.add), [expv])
            rinvv = p.op_d("vector", lambda e, pw=pw: e.reciprocal(
                rinv[:, pw:pw + 1], mcol[:, pw * 2: pw * 2 + 1]), [])
            trans = p.op_d("tensor", lambda e, pw=pw: e.transpose(
                psT[:], pbf[:, pw * 128:(pw + 1) * 128], idm[:]), [expv, ptcopy_hist])
            ptcopy_hist = p.op_d("vector", lambda e, pw=pw: e.tensor_copy(
                ptb[:, pw * 128:(pw + 1) * 128], psT[:]), [trans, omm_hist[pw]])
            trans_hist[pw] = trans
            exp_hist = expv
            j = w // 2
            par = j % 2
            last_omm = None
            if w % 2 == 0:
                for cb in range(2):
                    omm = p.op_d("tensor", lambda e, cb=cb, pw=pw: e.matmul(
                        psBs[cb][:], ptb[:, pw * 128:(pw + 1) * 128],
                        vsb[:, pw * DM + cb * 512: pw * DM + cb * 512 + 512],
                        start=True, stop=True), [ptcopy_hist, vcopy_hist[cb], sttr_hist[cb]])
                    sttr_hist[cb] = p.op_d("vector", lambda e, cb=cb, par=par, pw=pw: e.scalar_tensor_tensor(
                        acc[:, par * DM + cb * 512: par * DM + cb * 512 + 512],
                        psBs[cb][:], rinv[:, pw:pw + 1],
                        acc[:, par * DM + cb * 512: par * DM + cb * 512 + 512],
                        ALU.mult, ALU.add), [omm, rinvv])
                    last_omm = omm
            else:
                rr1 = p.op_d("sync", lambda e, pw=pw: e.dma_start(
                    out=rrot[64:128, pw:pw + 1], in_=rinv[0:64, pw:pw + 1]), [rinvv], inc=16)
                rr2 = p.op_d("sync", lambda e, pw=pw: e.dma_start(
                    out=rrot[0:64, pw:pw + 1], in_=rinv[64:128, pw:pw + 1]), [rinvv], inc=16)
                for cb in range(2):
                    p.op_d("tensor", lambda e, cb=cb, pw=pw: e.matmul(
                        psBs[cb][0:64, :], ptb[:, pw * 128 + 64: pw * 128 + 128],
                        vsb[:, pw * DM + cb * 512: pw * DM + cb * 512 + 512],
                        start=True, stop=True), [ptcopy_hist, vcopy_hist[cb], sttr_hist[cb]])
                    omm = p.op_d("tensor", lambda e, cb=cb, pw=pw: e.matmul(
                        psBs[cb][64:128, :], ptb[:, pw * 128: pw * 128 + 64],
                        vsb[:, pw * DM + cb * 512: pw * DM + cb * 512 + 512],
                        start=True, stop=True), [])
                    s1 = p.op_d("vector", lambda e, cb=cb, par=par, pw=pw: e.scalar_tensor_tensor(
                        acc[64:128, par * DM + cb * 512: par * DM + cb * 512 + 512],
                        psBs[cb][64:128, :], rrot[64:128, pw:pw + 1],
                        acc[64:128, par * DM + cb * 512: par * DM + cb * 512 + 512],
                        ALU.mult, ALU.add), [omm, rr1, rr2])
                    sttr_hist[cb] = p.op_d("vector", lambda e, cb=cb, par=par, pw=pw: e.scalar_tensor_tensor(
                        acc[0:64, (1 - par) * DM + cb * 512: (1 - par) * DM + cb * 512 + 512],
                        psBs[cb][0:64, :], rrot[0:64, pw:pw + 1],
                        acc[0:64, (1 - par) * DM + cb * 512: (1 - par) * DM + cb * 512 + 512],
                        ALU.mult, ALU.add), [])
                    last_omm = omm
                finalize(j, par)
            omm_hist[pw] = last_omm
            smm_hist[pw] = smm
        finalize(NTILE - 1, (NTILE - 1) % 2)



def build_fused():
    nc = bass.Bass("TRN2", target_bir_lowering=False)
    st = {}
    st["x"] = nc.dram_tensor("x", [T, DM], F32, kind="ExternalInput")
    st["xo"] = nc.dram_tensor("xo", [T, DM], F32, kind="ExternalOutput")
    for i in range(1, 6):
        st[f"xi{i}"] = nc.dram_tensor(f"xi{i}", [T, DM], F32, kind="Internal")
    st["W1"] = nc.dram_tensor("W1", [8, DM, 1536], F32, kind="ExternalInput")
    st["W2"] = nc.dram_tensor("W2", [8, 1536, DM], F32, kind="ExternalInput")
    st["W3"] = nc.dram_tensor("W3", [8, DM, 1536], F32, kind="ExternalInput")
    st["Wr"] = nc.dram_tensor("Wr", [DM, 8], F32, kind="ExternalInput")
    st["w1b"] = nc.dram_tensor("w1b", [8, DM, 1536], BF16, kind="Internal")
    st["w2b"] = nc.dram_tensor("w2b", [8, 1536, DM], BF16, kind="Internal")
    st["w3b"] = nc.dram_tensor("w3b", [8, DM, 1536], BF16, kind="Internal")
    st["xn_scr"] = nc.dram_tensor("xn_scr", [128, KC * T], BF16, kind="Internal")
    st["k123T"] = nc.dram_tensor("k123T", [DM, 96], F32, kind="ExternalInput")
    st["abc"] = nc.dram_tensor("abc", [128, 48], F32, kind="ExternalInput")
    st["vwT"] = nc.dram_tensor("vwT", [DM, DM], F32, kind="ExternalInput")
    st["owT"] = nc.dram_tensor("owT", [DM, DM], F32, kind="ExternalInput")
    st["nb128"] = nc.dram_tensor("nb128", [128, T], F32, kind="ExternalInput")
    st["selB"] = nc.dram_tensor("selB", [16, KC * 128], F32, kind="ExternalInput")
    st["blk16"] = nc.dram_tensor("blk16", [128, KC * 16], BF16, kind="ExternalInput")
    st["WfcT"] = nc.dram_tensor("WfcT", [DM, 4096], F32, kind="ExternalInput")
    st["WprT"] = nc.dram_tensor("WprT", [4096, DM], F32, kind="ExternalInput")
    st["Wq"] = nc.dram_tensor("Wq", [DM, DM], F32, kind="ExternalInput")
    st["Wk"] = nc.dram_tensor("Wk", [DM, DM], F32, kind="ExternalInput")
    st["Wv"] = nc.dram_tensor("Wv", [DM, DM], F32, kind="ExternalInput")
    st["Wo"] = nc.dram_tensor("Wo", [DM, DM], F32, kind="ExternalInput")
    st["bq"] = nc.dram_tensor("bq", [128, KC], F32, kind="ExternalInput")
    st["bk"] = nc.dram_tensor("bk", [128, KC], F32, kind="ExternalInput")
    st["bvr"] = nc.dram_tensor("bvr", [1, DM], F32, kind="ExternalInput")
    st["bor"] = nc.dram_tensor("bor", [1, DM], F32, kind="ExternalInput")
    st["cnt"] = nc.dram_tensor("cnt", [128, 3], F32, kind="ExternalInput")
    st["idm"] = nc.dram_tensor("idm", [128, 128], BF16, kind="ExternalInput")
    st["q_scr"] = nc.dram_tensor("q_scr", [128, KC * T], BF16, kind="Internal")
    st["k_scr"] = nc.dram_tensor("k_scr", [128, KC * T], BF16, kind="Internal")

    pool_bf = nc.alloc_sbuf_tensor("pool", [128, POOL_BYTES // 2], BF16)
    pool_f32 = pool_bf.bitcast(F32)
    ps = {
        "P1": nc.psum_tensor("P1", [128, 512], F32).__enter__(),
        "P2": nc.psum_tensor("P2", [128, 512], F32).__enter__(),
        "P3": nc.psum_tensor("P3", [128, 512], F32).__enter__(),
        "P4": nc.psum_tensor("P4", [128, 512], F32).__enter__(),
        "P5": nc.psum_tensor("P5", [128, 128], F32).__enter__(),
        "P6": nc.psum_tensor("P6", [128, 128], BF16).__enter__(),
        "P7": nc.psum_tensor("P7", [128, 512], F32).__enter__(),
        "P8": nc.psum_tensor("P8", [128, 512], F32).__enter__(),
    }
    p = SeqProg(nc)

    def pa():
        return PoolAlloc(pool_bf, pool_f32)

    p.serial_mode = not DEP_MOE
    emit_moe(p, st, ps, pa(), st["x"], st["xi1"], cast_weights=True)
    p.serial_mode = False
    emit_lat(p, st, ps, pa(), st["xi1"], st["xi2"])
    p.serial_mode = not DEP_MOE
    emit_moe(p, st, ps, pa(), st["xi2"], st["xi3"], cast_weights=False)
    p.serial_mode = not DEP_SFFN
    emit_sffn(p, st, ps, pa(), st["xi3"], st["xi4"])
    p.serial_mode = False
    emit_watt(p, st, ps, pa(), st["xi4"], st["xi5"])
    p.serial_mode = not DEP_SFFN
    emit_sffn(p, st, ps, pa(), st["xi5"], st["xo"])
    p.serial_mode = False
    p.emit()
    return nc



def _host_prep(inputs):
    d = {}
    k123T = np.zeros((DM, 96), np.float32)
    k123T[:, 0:16] = inputs["L0_attn_k1"].T
    k123T[:, 32:48] = inputs["L0_attn_k2"].T
    k123T[:, 64:80] = inputs["L0_attn_k3"].T
    d["k123T"] = k123T
    abc = np.zeros((6, KC, 128), np.float32)
    for nm, idx in (("a1", 0), ("b1", 1), ("a2", 2), ("b2", 3), ("c", 4)):
        abc[idx] = inputs["L0_attn_" + nm][:, :, 0].reshape(KC, 128)
    d["abc"] = np.ascontiguousarray(abc.transpose(2, 0, 1).reshape(128, 48))
    d["vwT"] = np.ascontiguousarray(
        inputs["L0_attn_value_weight"].transpose(1, 0, 2).reshape(DM, DM))
    d["owT"] = np.ascontiguousarray(
        inputs["L0_attn_output_weight"].transpose(0, 2, 1).reshape(DM, DM))
    nlin = np.linspace(0.0, 1.0, T, dtype=np.float32)
    d["nb128"] = np.broadcast_to(nlin[None, :], (128, T)).copy()
    selB = np.zeros((KC, 16, 128), np.float32)
    blk16 = np.zeros((KC, 128, 16), np.float32)
    for hp in range(KC):
        for r in range(2):
            selB[hp, 2 * hp + r, 64 * r:64 * (r + 1)] = 1.0
            blk16[hp, 64 * r:64 * (r + 1), 2 * hp + r] = 1.0
    d["selB"] = np.ascontiguousarray(selB.transpose(1, 0, 2).reshape(16, KC * 128))
    d["blk16"] = np.ascontiguousarray(blk16.transpose(1, 0, 2).reshape(128, KC * 16)).astype(BF)
    d["WfcT"] = np.ascontiguousarray(inputs["L1_ffn_W_fc"].T)
    d["WprT"] = np.ascontiguousarray(inputs["L1_ffn_W_proj"].T)
    d["bq"] = np.ascontiguousarray(inputs["L1_attn_bq"].reshape(KC, 128).T)
    d["bk"] = np.ascontiguousarray(inputs["L1_attn_bk"].reshape(KC, 128).T)
    d["bvr"] = inputs["L1_attn_bv"].reshape(1, DM).copy()
    d["bor"] = inputs["L1_attn_bo"].reshape(1, DM).copy()
    cnt = np.full((3, 128), 0.5, np.float32)
    cnt[0, 0:64] = 1.0
    cnt[2, 64:128] = 1.0
    d["cnt"] = np.ascontiguousarray(cnt.T)
    d["idm"] = np.eye(128).astype(BF)
    return d


_PROGS = {}


def _get_fused():
    if "fused" not in _PROGS:
        _PROGS["fused"] = build_fused()
    return _PROGS["fused"]


def kernel(**inputs):
    hp = _host_prep(inputs)
    B = 8
    x = np.ascontiguousarray(inputs["L0_attn_x"].astype(np.float32))
    extra = {"W1": inputs["L0_ffn_W1"], "W2": inputs["L0_ffn_W2"],
             "W3": inputs["L0_ffn_W3"], "Wr": inputs["L0_ffn_Wr"],
             "k123T": hp["k123T"], "abc": hp["abc"], "vwT": hp["vwT"],
             "owT": hp["owT"], "nb128": hp["nb128"], "selB": hp["selB"],
             "blk16": hp["blk16"],
             "WfcT": hp["WfcT"], "WprT": hp["WprT"],
             "Wq": inputs["L1_attn_Wq"], "Wk": inputs["L1_attn_Wk"],
             "Wv": inputs["L1_attn_Wv"], "Wo": inputs["L1_attn_Wo"],
             "bq": hp["bq"], "bk": hp["bk"], "bvr": hp["bvr"],
             "bor": hp["bor"], "cnt": hp["cnt"], "idm": hp["idm"]}
    in_maps = []
    for b in range(B):
        m = {"x": np.ascontiguousarray(x[b])}
        m.update(extra)
        in_maps.append(m)
    res = run_bass_kernel_spmd(_get_fused(), in_maps, core_ids=list(range(B)))
    return np.stack([np.asarray(r["xo"]) for r in res.results], axis=0)


# version 06 (final)
# speedup vs baseline: 1.1794x; 1.1457x over previous
import contextlib
import numpy as np
import ml_dtypes
import concourse.bass as bass
import concourse.mybir as mybir
from concourse.bass_utils import run_bass_kernel_spmd

F32 = mybir.dt.float32
BF16 = mybir.dt.bfloat16
AF = mybir.ActivationFunctionType
ALU = mybir.AluOpType
AX = mybir.AxisListType
BF = ml_dtypes.bfloat16

ENGINES = ["sync", "tensor", "vector", "scalar", "gpsimd"]

DM = 1024
T = 4096
KC = DM // 128
NTILE = T // 128


class SeqProg:

    SEM_LIMIT = 20000

    def __init__(self, nc):
        self.nc = nc
        self.sems = [nc.alloc_semaphore("seq0")]
        self.base = 0
        self.total = 0
        self.carry = []
        self.items = []
        self.dsem = {e: nc.alloc_semaphore("d_" + e) for e in ENGINES}
        self.dcnt = {e: 0 for e in ENGINES}
        self.dwaited = {e: 0 for e in ENGINES}
        self.dep_carry = {e: None for e in ENGINES}
        self.dgen = {e: 0 for e in ENGINES}

    def _dep_rotate(self, eng):
        if self.dcnt[eng] > 50000:
            self.carry.append((self.dsem[eng], self.dcnt[eng]))
            self.dep_carry[eng] = (self.dsem[eng], self.dcnt[eng])
            self.dgen[eng] += 1
            self.dsem[eng] = self.nc.alloc_semaphore(f"d_{eng}{self.dgen[eng]}")
            self.dcnt[eng] = 0
            self.dwaited[eng] = 0

    def _rotate_if_needed(self):
        if self.total - self.base > self.SEM_LIMIT:
            self.carry.append((self.sems[-1], self.total - self.base))
            self.sems.append(self.nc.alloc_semaphore(f"seq{len(self.sems)}"))
            self.base = self.total

    def _pre(self):
        self._rotate_if_needed()
        waits = list(self.carry)
        self.carry = []
        if self.total > self.base:
            waits.append((self.sems[-1], self.total - self.base))
        for e in ENGINES:
            if self.dcnt[e] > self.dwaited[e]:
                waits.append((self.dsem[e], self.dcnt[e]))
                self.dwaited[e] = self.dcnt[e]
        return waits

    def op(self, eng, fn, inc=1):
        waits = self._pre()
        self.items.append((eng, waits, fn, (self.sems[-1], inc)))
        self.total += inc
        return (eng, self.sems[-1], self.total - self.base)

    def dma(self, eng, fn):
        return self.op(eng, fn, inc=16)

    def chain(self, eng, fn_incs):
        waits = self._pre()
        for i, (fn, inc) in enumerate(fn_incs):
            self.items.append((eng, waits if i == 0 else [], fn, (self.sems[-1], inc)))
            self.total += inc
        return (eng, self.sems[-1], self.total - self.base)

    def mm_chain(self, fns):
        return self.chain("tensor", [(f, 0) for f in fns[:-1]] + [(fns[-1], 1)])

    def dma_chain(self, fns):
        return self.chain("sync", [(f, 16) for f in fns])

    def _dep_waits(self, eng, deps):
        self._rotate_if_needed()
        waits = {}
        for d in deps:
            if d is None:
                continue
            (e2, s2, v2) = d
            if e2 == eng:
                continue
            if s2 in waits:
                waits[s2] = max(waits[s2], v2)
            else:
                waits[s2] = v2
        return [(s, v) for (s, v) in waits.items()]

    serial_mode = False

    def op_d(self, eng, fn, deps, inc=1):
        if self.serial_mode:
            return self.op(eng, fn, inc)
        self._dep_rotate(eng)
        waits = self._dep_waits(eng, deps)
        if self.dcnt[eng] > 0:
            waits.append((self.dsem[eng], self.dcnt[eng]))
        elif self.dep_carry[eng] is not None:
            waits.append(self.dep_carry[eng])
            self.dep_carry[eng] = None
        self.items.append((eng, waits, fn, (self.dsem[eng], inc)))
        self.dcnt[eng] += inc
        return (eng, self.dsem[eng], self.dcnt[eng])

    def chain_d(self, eng, fn_incs, deps):
        if self.serial_mode:
            return self.chain(eng, fn_incs)
        self._dep_rotate(eng)
        waits = self._dep_waits(eng, deps)
        if self.dcnt[eng] > 0:
            waits.append((self.dsem[eng], self.dcnt[eng]))
        elif self.dep_carry[eng] is not None:
            waits.append(self.dep_carry[eng])
            self.dep_carry[eng] = None
        for i, (fn, inc) in enumerate(fn_incs):
            self.items.append((eng, waits if i == 0 else [], fn, (self.dsem[eng], inc)))
            self.dcnt[eng] += inc
        return (eng, self.dsem[eng], self.dcnt[eng])

    def mm_chain_d(self, fns, deps):
        return self.chain_d("tensor", [(f, 0) for f in fns[:-1]] + [(fns[-1], 1)], deps)

    def dma_chain_d(self, fns, deps):
        return self.chain_d("sync", [(f, 16) for f in fns], deps)

    def emit(self):
        final_sem = self.sems[-1]
        final_val = self.total - self.base
        final_carry = list(self.carry)
        with self.nc.Block() as blk:
            for name in ENGINES:
                lst = [(w, f, u) for (e, w, f, u) in self.items if e == name]
                if not lst and name != "sync":
                    continue

                def body(eng, lst=lst, name=name):
                    for (waits, f, (sem, inc)) in lst:
                        for (s, v) in waits:
                            eng.wait_ge(s, v)
                        inst = f(eng)
                        if inc:
                            inst.then_inc(sem, inc)
                    if name == "sync":
                        for (s, v) in final_carry:
                            eng.wait_ge(s, v)
                        for e2 in ENGINES:
                            if self.dcnt[e2] > 0:
                                eng.wait_ge(self.dsem[e2], self.dcnt[e2])
                        eng.wait_ge(final_sem, final_val)
                getattr(blk, name)(body)


class TView:

    def __init__(self, h, pn, c0, cn):
        self.h = h
        self.pn = pn
        self.c0 = c0
        self.cn = cn

    def _ps(self, s):
        start = 0 if s.start is None else s.start
        stop = self.pn if s.stop is None else s.stop
        return slice(start, stop)

    def _cs(self, s):
        start = 0 if s.start is None else s.start
        stop = self.cn if s.stop is None else s.stop
        return slice(self.c0 + start, self.c0 + stop)

    def __getitem__(self, idx):
        if not isinstance(idx, tuple):
            idx = (idx, slice(None))
        return self.h[self._ps(idx[0]), self._cs(idx[1])]


POOL_BYTES = 199 * 1024
DEP_MOE = True
DEP_SFFN = True


class PoolAlloc:

    def __init__(self, bf_h, f32_h):
        self.bf = bf_h
        self.f32 = f32_h
        self.off = 0

    def t(self, shape, dtype):
        assert len(shape) == 2
        esz = 2 if dtype == BF16 else 4
        per_b = shape[1] * esz
        off = self.off
        self.off = (off + per_b + 31) & ~31
        assert self.off <= POOL_BYTES, f"pool overflow: {self.off}"
        h = self.bf if dtype == BF16 else self.f32
        return TView(h, shape[0], off // esz, shape[1])


def emit_ln(p, x_ap, tile, xt, xcf, sq, xnt, cols):
    mean = cols[:, 0:1]
    var = cols[:, 1:2]
    rstd = cols[:, 2:3]
    p.dma("sync", lambda e: e.dma_start(out=xt[:], in_=x_ap[tile * 128:(tile + 1) * 128, :]))
    p.op("vector", lambda e: e.tensor_reduce(mean, xt[:], AX.X, ALU.add))
    p.op("vector", lambda e: e.tensor_scalar_mul(mean, mean, 1.0 / DM))
    p.op("vector", lambda e: e.tensor_scalar_sub(xcf[:], xt[:], mean))
    p.op("scalar", lambda e: e.activation(out=sq[:, 0:DM], in_=xcf[:], func=AF.Square))
    p.op("vector", lambda e: e.tensor_reduce(var, sq[:, 0:DM], AX.X, ALU.add))
    p.op("vector", lambda e: e.tensor_scalar(var, var, 1.0 / DM, 1e-5, ALU.mult, ALU.add))
    p.op("scalar", lambda e: e.activation(out=var, in_=var, func=AF.Sqrt))
    p.op("vector", lambda e: e.reciprocal(rstd, var))
    p.op("vector", lambda e: e.tensor_scalar_mul(xnt[:], xcf[:], rstd))


def emit_xnf_tile(p, xnt, xnf, ttl, stride):
    p.dma_chain([
        (lambda e, k=k: e.dma_start_transpose(
            out=xnf[:, k * stride + ttl * 128: k * stride + ttl * 128 + 128],
            in_=xnt[:, k * 128:(k + 1) * 128]))
        for k in range(KC)])



def emit_moe(p, st, ps, pa, x_ap, xo_ap, cast_weights):
    E, HID = 8, 1536
    MC = HID // 128
    w1_h, w2_h, w3_h, wr_h = st["W1"], st["W2"], st["W3"], st["Wr"]
    w1b_s, w2b_s, w3b_s, xn_s = st["w1b"], st["w2b"], st["w3b"], st["xn_scr"]
    if True:
        A = pa.t
        acc = A([128, 8 * 1024], F32)
        xnf = A([128, KC * 1024], BF16)
        w1sb = A([128, KC * HID], BF16)
        w3sb = A([128, KC * HID], BF16)
        w2sb = A([128, MC * DM], BF16)
        wrsb = A([128, KC * E], BF16)
        stg = A([128, HID], F32)
        stgb = A([128, HID], BF16)
        xt = A([128, DM], F32)
        xcf = A([128, DM], F32)
        xnt = A([128, DM], BF16)
        comb = A([128, NTILE * E], F32)
        hs = A([128, 2 * 512], BF16)
        gbuf = A([128, 2 * 512], BF16)
        z = A([128, 2 * MC * 512], BF16)
        cols = A([128, 8], F32)
        lg = A([128, E], F32)
        tmp = A([128, E], F32)
        tmp2 = A([128, E], F32)
        mask1 = A([128, E], F32)
        mask2 = A([128, E], F32)
        c1 = A([128, E], F32)
        gc = A([128, 8], F32)
        psH = ps["P1"]
        psG = ps["P2"]
        psO = ps["P3"]
        psL = ps["P5"]

        if cast_weights:
            stg2 = A([128, HID], F32)
            stgb2 = A([128, HID], BF16)
            stgs = [stg, stg2]
            stgbs = [stgb, stgb2]
            cp_hist = [None, None]
            stw_hist = [None, None]
            i = 0
            for ex in range(E):
                for (src, dst, width, nchunk) in ((w1_h, w1b_s, HID, KC), (w3_h, w3b_s, HID, KC), (w2_h, w2b_s, DM, MC)):
                    for k in range(nchunk):
                        par = i % 2
                        ldt = p.op_d("sync", lambda e, src=src, ex=ex, k=k, width=width, par=par: e.dma_start(
                            out=stgs[par][:, 0:width], in_=src[ex, k * 128:(k + 1) * 128, :]),
                            [cp_hist[par]], inc=16)
                        cp_hist[par] = p.op_d("vector", lambda e, width=width, par=par: e.tensor_copy(
                            stgbs[par][:, 0:width], stgs[par][:, 0:width]),
                            [ldt, stw_hist[par]])
                        stw_hist[par] = p.op_d("gpsimd", lambda e, dst=dst, ex=ex, k=k, width=width, par=par: e.dma_start(
                            out=dst[ex, k * 128:(k + 1) * 128, :], in_=stgbs[par][:, 0:width]),
                            [cp_hist[par]], inc=16)
                        i += 1
        for k in range(KC):
            p.dma("sync", lambda e, k=k: e.dma_start(out=stg[:, 0:E], in_=wr_h[k * 128:(k + 1) * 128, :]))
            p.op("vector", lambda e, k=k: e.tensor_copy(wrsb[:, k * E:(k + 1) * E], stg[:, 0:E]))

        for ps4 in range(4):
            for ttl in range(8):
                tile = ps4 * 8 + ttl
                emit_ln(p, x_ap, tile, xt, xcf, stg, xnt, cols)
                emit_xnf_tile(p, xnt, xnf, ttl, 1024)
                p.mm_chain([
                    (lambda e, k=k, ttl=ttl: e.matmul(
                        psL[:, 0:E],
                        xnf[:, k * 1024 + ttl * 128: k * 1024 + ttl * 128 + 128],
                        wrsb[:, k * E:(k + 1) * E],
                        start=(k == 0), stop=(k == KC - 1)))
                    for k in range(KC)])
                m1 = gc[:, 0:1]; m2 = gc[:, 1:2]; e2 = gc[:, 2:3]
                dd = gc[:, 3:4]; w1h = gc[:, 4:5]; w2h = gc[:, 5:6]
                p.op("vector", lambda e: e.tensor_copy(lg[:], psL[:, 0:E]))
                p.op("vector", lambda e: e.tensor_reduce(m1, lg[:], AX.X, ALU.max))
                p.op("vector", lambda e: e.tensor_scalar_sub(tmp[:], lg[:], m1))
                p.op("vector", lambda e: e.tensor_single_scalar(mask1[:], tmp[:], 0.0, ALU.is_equal))
                p.op("vector", lambda e: e.scalar_tensor_tensor(tmp2[:], mask1[:], -1e30, tmp[:], ALU.mult, ALU.add))
                p.op("vector", lambda e: e.tensor_reduce(m2, tmp2[:], AX.X, ALU.max))
                p.op("scalar", lambda e: e.activation(out=e2, in_=m2, func=AF.Exp))
                p.op("vector", lambda e: e.tensor_scalar_add(dd, e2, 1.0))
                p.op("vector", lambda e: e.reciprocal(w1h, dd))
                p.op("vector", lambda e: e.tensor_scalar_mul(w1h, w1h, 0.5))
                p.op("vector", lambda e: e.tensor_scalar(w2h, w1h, -1.0, 0.5, ALU.mult, ALU.add))
                p.op("vector", lambda e: e.tensor_single_scalar(mask2[:], tmp2[:], m2, ALU.is_equal))
                p.op("vector", lambda e: e.tensor_scalar_mul(c1[:], mask1[:], w1h))
                p.op("vector", lambda e: e.tensor_scalar_mul(mask2[:], mask2[:], w2h))
                p.op("vector", lambda e, tile=tile: e.tensor_add(comb[:, tile * E:(tile + 1) * E], c1[:], mask2[:]))
            p.dma_chain([
                (lambda e, k=k, ps4=ps4: e.dma_start(
                    out=xn_s[:, k * T + ps4 * 1024: k * T + ps4 * 1024 + 1024],
                    in_=xnf[:, k * 1024:(k + 1) * 1024]))
                for k in range(KC)])

        psHs = [ps["P1"], ps["P4"]]
        psGs = [ps["P2"], ps["P7"]]
        psOs = [ps["P3"], ps["P8"]]

        def w13_load(ex, deps):
            return p.dma_chain_d(
                [(lambda e, k=k, ex=ex: e.dma_start(
                    out=w1sb[:, k * HID:(k + 1) * HID], in_=w1b_s[ex, k * 128:(k + 1) * 128, :]))
                 for k in range(KC)] +
                [(lambda e, k=k, ex=ex: e.dma_start(
                    out=w3sb[:, k * HID:(k + 1) * HID], in_=w3b_s[ex, k * 128:(k + 1) * 128, :]))
                 for k in range(KC)], deps)

        def w2_load(ex, deps):
            return p.dma_chain_d(
                [(lambda e, k=k, ex=ex: e.dma_start(
                    out=w2sb[:, k * DM:(k + 1) * DM], in_=w2b_s[ex, k * 128:(k + 1) * 128, :]))
                 for k in range(MC)], deps)

        sil_hist = [None, None]
        cpy_hist = [None, None]
        stt_hist = [None, None]
        prev_w2_read = None
        w13_tok = None
        w2_tok = None
        for ps4 in range(4):
            p.op("vector", lambda e: e.memset(acc[:], 0.0))
            xnf_tok = p.dma_chain([
                (lambda e, k=k, ps4=ps4: e.dma_start(
                    out=xnf[:, k * 1024:(k + 1) * 1024],
                    in_=xn_s[:, k * T + ps4 * 1024: k * T + ps4 * 1024 + 1024]))
                for k in range(KC)])
            if ps4 == 0:
                w13_tok = w13_load(0, [])
                w2_tok = w2_load(0, [])
            for ex in range(E):
                mmg_last = None
                tt_last = [None, None]
                for m in range(2 * MC):
                    nt2, ml = m // MC, m % MC
                    tb = nt2 * 512
                    par = m % 2
                    deps_h = [w13_tok, sil_hist[par]]
                    if m == 0:
                        deps_h.append(xnf_tok)
                    mmh = p.mm_chain_d([
                        (lambda e, k=k, ml=ml, tb=tb, par=par: e.matmul(
                            psHs[par][:], w1sb[:, k * HID + ml * 128: k * HID + ml * 128 + 128],
                            xnf[:, k * 1024 + tb: k * 1024 + tb + 512],
                            start=(k == 0), stop=(k == KC - 1)))
                        for k in range(KC)], deps_h)
                    sil = p.op_d("scalar", lambda e, par=par: e.activation(
                        out=hs[:, par * 512:(par + 1) * 512], in_=psHs[par][:], func=AF.Silu), [mmh])
                    mmg = p.mm_chain_d([
                        (lambda e, k=k, ml=ml, tb=tb, par=par: e.matmul(
                            psGs[par][:], w3sb[:, k * HID + ml * 128: k * HID + ml * 128 + 128],
                            xnf[:, k * 1024 + tb: k * 1024 + tb + 512],
                            start=(k == 0), stop=(k == KC - 1)))
                        for k in range(KC)], [cpy_hist[par]])
                    cpy = p.op_d("vector", lambda e, par=par: e.tensor_copy(
                        gbuf[:, par * 512:(par + 1) * 512], psGs[par][:]), [mmg])
                    deps_tt = [sil]
                    if m == 0:
                        deps_tt.append(prev_w2_read)
                    ttok = p.op_d("vector", lambda e, m=m, par=par: e.tensor_mul(
                        z[:, m * 512:(m + 1) * 512],
                        hs[:, par * 512:(par + 1) * 512],
                        gbuf[:, par * 512:(par + 1) * 512]), deps_tt)
                    sil_hist[par] = ttok
                    cpy_hist[par] = ttok
                    tt_last[nt2] = ttok
                    mmg_last = mmg
                nxt = ex + 1 if ex < E - 1 else (0 if ps4 < 3 else None)
                if nxt is not None:
                    w13_next = w13_load(nxt, [mmg_last])
                mmo_last = None
                for it in range(16):
                    nt2, tq, cb = it // 8, (it // 2) % 4, it % 2
                    ttl = nt2 * 4 + tq
                    tile = ps4 * 8 + ttl
                    par = it % 2
                    mmo = p.mm_chain_d([
                        (lambda e, k=k, nt2=nt2, tq=tq, cb=cb, par=par: e.matmul(
                            psOs[par][:], z[:, (nt2 * MC + k) * 512 + tq * 128: (nt2 * MC + k) * 512 + tq * 128 + 128],
                            w2sb[:, k * DM + cb * 512: k * DM + cb * 512 + 512],
                            start=(k == 0), stop=(k == MC - 1)))
                        for k in range(MC)], [tt_last[nt2], w2_tok, stt_hist[par]])
                    stt_hist[par] = p.op_d("vector", lambda e, ttl=ttl, cb=cb, ex=ex, tile=tile, par=par: e.scalar_tensor_tensor(
                        acc[:, ttl * 1024 + cb * 512: ttl * 1024 + cb * 512 + 512],
                        psOs[par][:], comb[:, tile * E + ex: tile * E + ex + 1],
                        acc[:, ttl * 1024 + cb * 512: ttl * 1024 + cb * 512 + 512],
                        ALU.mult, ALU.add), [mmo])
                    mmo_last = mmo
                prev_w2_read = mmo_last
                if nxt is not None:
                    w13_tok = w13_next
                    w2_tok = w2_load(nxt, [mmo_last])
            for ttl in range(8):
                tile = ps4 * 8 + ttl
                p.dma("sync", lambda e, tile=tile: e.dma_start(out=xt[:], in_=x_ap[tile * 128:(tile + 1) * 128, :]))
                p.op("vector", lambda e, ttl=ttl: e.tensor_add(xcf[:], xt[:], acc[:, ttl * 1024:(ttl + 1) * 1024]))
                p.dma("sync", lambda e, tile=tile: e.dma_start(out=xo_ap[tile * 128:(tile + 1) * 128, :], in_=xcf[:]))



def emit_lat(p, st, ps, pa, x_ap, xo_ap):
    k123_h, abc_h, vwT_h, owT_h = st["k123T"], st["abc"], st["vwT"], st["owT"]
    nb_h, sel_h, blk_h = st["nb128"], st["selB"], st["blk16"]
    if True:
        A = pa.t
        xnf = A([128, KC * 512], BF16)
        k123 = A([128, KC * 96], BF16)
        vw = A([128, KC * DM], BF16)
        ow = A([128, KC * DM], BF16)
        nb = A([128, T], F32)
        selB = A([16, KC * 128], F32)
        blk16 = A([128, KC * 16], BF16)
        abc = A([128, 48], F32)
        stg = A([128, DM], F32)
        p1r = A([16, T], F32)
        p2er = A([16, T], F32)
        sinm = A([128, 512], F32)
        scm = A([128, 512], BF16)
        xt = A([128, DM], F32)
        xcf = A([128, DM], F32)
        xnt = A([128, DM], BF16)
        cols = A([128, 8], F32)
        crs = A([16, 512], F32)
        wr_ = A([16, 512], F32)
        dr_ = A([16, 512], F32)
        rden = A([16, 512], F32)
        cumc = A([16, 512], F32)
        ones16 = A([16, 512], F32)
        carr_c = A([16, 1], F32)
        vf = A([128, KC * 512], F32)
        ones128 = A([128, 512], F32)
        ceb = A([128, 2 * 512], F32)
        wb = A([128, 2 * 512], F32)
        db = A([128, 2 * 512], F32)
        rdb = A([128, 2 * 512], F32)
        cv = A([128, 2 * 512], F32)
        cumv = A([128, 2 * 512], F32)
        carr_v = A([128, KC], F32)
        o1 = A([128, 2 * 512], F32)
        o2 = A([128, 2 * 512], F32)
        obf = A([128, KC * 512], BF16)
        psA = ps["P1"]
        psB = ps["P2"]
        psC = TView(ps["P3"], 96, 0, 512)
        ps16 = TView(ps["P4"], 16, 0, 512)

        p.dma("sync", lambda e: e.dma_start(out=nb[:], in_=nb_h[:]))
        p.dma("sync", lambda e: e.dma_start(out=selB[:], in_=sel_h[:]))
        p.dma("sync", lambda e: e.dma_start(out=blk16[:], in_=blk_h[:]))
        p.dma("sync", lambda e: e.dma_start(out=abc[:], in_=abc_h[:]))
        for k in range(KC):
            p.dma("sync", lambda e, k=k: e.dma_start(out=stg[:, 0:96], in_=k123_h[k * 128:(k + 1) * 128, :]))
            p.op("vector", lambda e, k=k: e.tensor_copy(k123[:, k * 96:(k + 1) * 96], stg[:, 0:96]))
            p.dma("sync", lambda e, k=k: e.dma_start(out=stg[:], in_=vwT_h[k * 128:(k + 1) * 128, :]))
            p.op("vector", lambda e, k=k: e.tensor_copy(vw[:, k * DM:(k + 1) * DM], stg[:]))
            p.dma("sync", lambda e, k=k: e.dma_start(out=stg[:], in_=owT_h[k * 128:(k + 1) * 128, :]))
            p.op("vector", lambda e, k=k: e.tensor_copy(ow[:, k * DM:(k + 1) * DM], stg[:]))
        p.op("vector", lambda e: e.memset(ones16[:], 1.0))
        p.op("vector", lambda e: e.memset(ones128[:], 1.0))
        p.op("vector", lambda e: e.memset(carr_c[:], 0.0))
        p.op("vector", lambda e: e.memset(carr_v[:], 0.0))

        for nt in range(8):
            sl = slice(nt * 512, nt * 512 + 512)
            for which in range(2):
                for hp in range(KC):
                    ai = (0 if which == 0 else 2) * KC + hp
                    bi = (1 if which == 0 else 3) * KC + hp
                    ci = 4 * KC + hp
                    p.op("scalar", lambda e, ai=ai, bi=bi, sl=sl: e.activation(
                        out=sinm[:], in_=nb[:, sl], func=AF.Sin,
                        bias=abc[:, bi:bi + 1], scale=abc[:, ai:ai + 1]))
                    p.op("vector", lambda e, ci=ci: e.tensor_scalar_mul(scm[:], sinm[:], abc[:, ci:ci + 1]))
                    p.op("tensor", lambda e, hp=hp: e.matmul(
                        ps16[:], blk16[:, hp * 16:(hp + 1) * 16], scm[:],
                        start=(hp == 0), stop=(hp == KC - 1)))
                if which == 0:
                    p.op("vector", lambda e, sl=sl: e.tensor_copy(p1r[:, sl], ps16[:]))
                else:
                    p.op("vector", lambda e: e.tensor_scalar(crs[:], ps16[:], 20.0, -20.0, ALU.min, ALU.max))
                    p.op("scalar", lambda e, sl=sl: e.activation(out=p2er[:, sl], in_=crs[:], func=AF.Exp))

        for nt in range(8):
            sl = slice(nt * 512, nt * 512 + 512)
            for tq in range(4):
                tile = nt * 4 + tq
                emit_ln(p, x_ap, tile, xt, xcf, stg, xnt, cols)
                emit_xnf_tile(p, xnt, xnf, tq, 512)
            p.mm_chain([
                (lambda e, k=k: e.matmul(
                    psC[:], k123[:, k * 96:(k + 1) * 96], xnf[:, k * 512:(k + 1) * 512],
                    start=(k == 0), stop=(k == KC - 1)))
                for k in range(KC)])
            p.op("vector", lambda e, sl=sl: e.tensor_add(crs[:], psC[0:16, :], p1r[:, sl]))
            p.op("vector", lambda e: e.tensor_scalar(crs[:], crs[:], 20.0, -20.0, ALU.min, ALU.max))
            p.op("scalar", lambda e: e.activation(out=crs[:], in_=crs[:], func=AF.Exp))
            p.op("vector", lambda e: e.tensor_scalar(dr_[:], psC[32:48, :], 20.0, -20.0, ALU.min, ALU.max))
            p.op("scalar", lambda e: e.activation(out=dr_[:], in_=dr_[:], func=AF.Exp))
            p.op("vector", lambda e: e.tensor_scalar(wr_[:], psC[64:80, :], 20.0, -20.0, ALU.min, ALU.max))
            p.op("scalar", lambda e: e.activation(out=wr_[:], in_=wr_[:], func=AF.Exp))
            p.op("vector", lambda e, sl=sl: e.tensor_mul(wr_[:], wr_[:], p2er[:, sl]))
            p.op("vector", lambda e: e.tensor_tensor_scan(cumc[:], ones16[:], crs[:], carr_c[:, 0:1], ALU.mult, ALU.add))
            p.op("vector", lambda e: e.tensor_copy(carr_c[:], cumc[:, 511:512]))
            p.op("vector", lambda e: e.tensor_mul(rden[:], cumc[:], wr_[:]))
            p.op("vector", lambda e: e.tensor_add(rden[:], rden[:], dr_[:]))
            p.op("vector", lambda e: e.reciprocal(rden[:], rden[:]))
            vf_tok = None
            for m in range(KC):
                p.mm_chain([
                    (lambda e, k=k, m=m: e.matmul(
                        psA[:], vw[:, k * DM + m * 128: k * DM + m * 128 + 128],
                        xnf[:, k * 512:(k + 1) * 512],
                        start=(k == 0), stop=(k == KC - 1)))
                    for k in range(KC)])
                vf_tok = p.op("vector", lambda e, m=m: e.tensor_copy(vf[:, m * 512:(m + 1) * 512], psA[:]))
            psP = [psB, psA]
            evac_ps_hist = [None, None]
            last_dve = [None, None]
            for hp in range(KC):
                ph = hp % 2
                po = ph * 512
                bcs = ((crs, ceb), (wr_, wb), (dr_, db), (rden, rdb))
                for i, (src, dst) in enumerate(bcs):
                    pi = i % 2
                    mmb = p.op_d("tensor", lambda e, hp=hp, src=src, pi=pi: e.matmul(
                        psP[pi][:], selB[:, hp * 128:(hp + 1) * 128], src[:],
                        start=True, stop=True), [vf_tok, evac_ps_hist[pi]])
                    evac_ps_hist[pi] = p.op_d("vector", lambda e, dst=dst, pi=pi, po=po: e.tensor_copy(
                        dst[:, po:po + 512], psP[pi][:]), [mmb, last_dve[ph]])
                vsl = vf[:, hp * 512:(hp + 1) * 512]
                p.op_d("vector", lambda e, vsl=vsl, po=po: e.tensor_mul(
                    cv[:, po:po + 512], ceb[:, po:po + 512], vsl), [])
                p.op_d("vector", lambda e, hp=hp, po=po: e.tensor_tensor_scan(
                    cumv[:, po:po + 512], ones128[:], cv[:, po:po + 512],
                    carr_v[:, hp:hp + 1], ALU.mult, ALU.add), [])
                p.op_d("vector", lambda e, hp=hp, po=po: e.tensor_copy(
                    carr_v[:, hp:hp + 1], cumv[:, po + 511:po + 512]), [])
                p.op_d("vector", lambda e, po=po: e.tensor_mul(
                    o1[:, po:po + 512], cumv[:, po:po + 512], wb[:, po:po + 512]), [])
                p.op_d("vector", lambda e, vsl=vsl, po=po: e.tensor_mul(
                    o2[:, po:po + 512], vsl, db[:, po:po + 512]), [])
                p.op_d("vector", lambda e, po=po: e.tensor_add(
                    o1[:, po:po + 512], o1[:, po:po + 512], o2[:, po:po + 512]), [])
                last_dve[ph] = p.op_d("vector", lambda e, hp=hp, po=po: e.tensor_mul(
                    obf[:, hp * 512:(hp + 1) * 512], o1[:, po:po + 512], rdb[:, po:po + 512]), [])
            for tq in range(4):
                tile = nt * 4 + tq
                p.dma("sync", lambda e, tile=tile: e.dma_start(out=xt[:], in_=x_ap[tile * 128:(tile + 1) * 128, :]))
                for cb in range(2):
                    p.mm_chain([
                        (lambda e, k=k, tq=tq, cb=cb: e.matmul(
                            psA[:], obf[:, k * 512 + tq * 128: k * 512 + tq * 128 + 128],
                            ow[:, k * DM + cb * 512: k * DM + cb * 512 + 512],
                            start=(k == 0), stop=(k == KC - 1)))
                        for k in range(KC)])
                    p.op("vector", lambda e, cb=cb: e.tensor_add(
                        xcf[:, cb * 512:(cb + 1) * 512], xt[:, cb * 512:(cb + 1) * 512], psA[:]))
                p.dma("sync", lambda e, tile=tile: e.dma_start(out=xo_ap[tile * 128:(tile + 1) * 128, :], in_=xcf[:]))



def emit_sffn(p, st, ps, pa, x_ap, xo_ap):
    FF = 4096
    FC = FF // 128
    wfcT_h, wprT_h = st["WfcT"], st["WprT"]
    if True:
        A = pa.t
        wfc = A([128, KC * FF], BF16)
        wpr = A([128, FC * DM], BF16)
        stg = A([128, DM], F32)
        xnf = A([128, KC * 512], BF16)
        hb = A([128, FC * 512], BF16)
        xt = A([128, DM], F32)
        xcf = A([128, DM], F32)
        xnt = A([128, DM], BF16)
        cols = A([128, 8], F32)
        relu = A([128, 2 * 512], F32)
        yb = A([128, 512], F32)
        mb = A([128, 512], F32)
        sb_ = A([128, 512], F32)
        ab = A([128, 512], F32)
        tb = A([128, 512], F32)
        psA = ps["P1"]
        psB = ps["P2"]

        for k in range(KC):
            for h2 in range(4):
                p.dma("sync", lambda e, k=k, h2=h2: e.dma_start(
                    out=stg[:], in_=wfcT_h[k * 128:(k + 1) * 128, h2 * 1024:(h2 + 1) * 1024]))
                p.op("vector", lambda e, k=k, h2=h2: e.tensor_copy(
                    wfc[:, k * FF + h2 * 1024: k * FF + (h2 + 1) * 1024], stg[:]))
        for k in range(FC):
            p.dma("sync", lambda e, k=k: e.dma_start(out=stg[:, 0:DM], in_=wprT_h[k * 128:(k + 1) * 128, :]))
            p.op("vector", lambda e, k=k: e.tensor_copy(wpr[:, k * DM:(k + 1) * DM], stg[:, 0:DM]))

        xn_s = st["xn_scr"]
        for nt in range(8):
            for tq in range(4):
                tile = nt * 4 + tq
                emit_ln(p, x_ap, tile, xt, xcf, stg, xnt, cols)
                emit_xnf_tile(p, xnt, xnf, tq, 512)
            p.dma_chain([
                (lambda e, k=k, nt=nt: e.dma_start(
                    out=xn_s[:, k * T + nt * 512: k * T + nt * 512 + 512],
                    in_=xnf[:, k * 512:(k + 1) * 512]))
                for k in range(KC)])

        psAs = [ps["P1"], ps["P4"]]
        psBs = [ps["P2"], ps["P7"]]
        relu_hist = [None, None]
        sq_hist = [None, None]
        copy_hist = [None, None]
        stt_prev = None
        store_prev = None
        mmf_last = None
        mmy_last = None
        for nt in range(8):
            lfns = [(lambda e, k=k, nt=nt: e.dma_start(
                out=xnf[:, k * 512:(k + 1) * 512],
                in_=xn_s[:, k * T + nt * 512: k * T + nt * 512 + 512]))
                for k in range(KC)]
            if nt == 0:
                xnf_tok = p.dma_chain(lfns)
            else:
                xnf_tok = p.dma_chain_d(lfns, [mmf_last])
            sq = None
            for f in range(FC):
                par = f % 2
                mmf = p.mm_chain_d([
                    (lambda e, k=k, f=f, par=par: e.matmul(
                        psAs[par][:], wfc[:, k * FF + f * 128: k * FF + f * 128 + 128],
                        xnf[:, k * 512:(k + 1) * 512],
                        start=(k == 0), stop=(k == KC - 1)))
                    for k in range(KC)], [xnf_tok, relu_hist[par]])
                rel = p.op_d("vector", lambda e, par=par: e.tensor_scalar_max(
                    relu[:, par * 512:(par + 1) * 512], psAs[par][:], 0.0),
                    [mmf, sq_hist[par]])
                sqdeps = [rel]
                if f == 0:
                    sqdeps.append(mmy_last)
                sq = p.op_d("scalar", lambda e, f=f, par=par: e.activation(
                    out=hb[:, f * 512:(f + 1) * 512], in_=relu[:, par * 512:(par + 1) * 512],
                    func=AF.Square), sqdeps)
                relu_hist[par] = rel
                sq_hist[par] = sq
                mmf_last = mmf
            for tq in range(4):
                tile = nt * 4 + tq
                xt_tok = p.op_d("sync", lambda e, tile=tile: e.dma_start(
                    out=xt[:], in_=x_ap[tile * 128:(tile + 1) * 128, :]), [stt_prev], inc=16)
                stt_cb = None
                for cb in range(2):
                    parB = cb
                    mmy = p.mm_chain_d([
                        (lambda e, k=k, tq=tq, cb=cb, parB=parB: e.matmul(
                            psBs[parB][:], hb[:, k * 512 + tq * 128: k * 512 + tq * 128 + 128],
                            wpr[:, k * DM + cb * 512: k * DM + cb * 512 + 512],
                            start=(k == 0), stop=(k == FC - 1)))
                        for k in range(FC)], [sq, copy_hist[parB]])
                    cpy = p.op_d("vector", lambda e, parB=parB: e.tensor_copy(yb[:], psBs[parB][:]), [mmy])
                    copy_hist[parB] = cpy
                    p.op_d("vector", lambda e: e.tensor_single_scalar(sb_[:], yb[:], 1.0, ALU.is_gt), [])
                    p.op_d("vector", lambda e: e.tensor_copy(ab[:], sb_[:]), [])
                    p.op_d("vector", lambda e: e.tensor_copy(mb[:], yb[:]), [])
                    for step in range(4):
                        p.op_d("vector", lambda e: e.scalar_tensor_tensor(tb[:], mb[:], 0.9, yb[:], ALU.mult, ALU.add), [])
                        p.op_d("vector", lambda e: e.tensor_sub(mb[:], tb[:], sb_[:]), [])
                        p.op_d("vector", lambda e: e.tensor_single_scalar(sb_[:], mb[:], 1.0, ALU.is_gt), [])
                        p.op_d("vector", lambda e: e.tensor_add(ab[:], ab[:], sb_[:]), [])
                    stt_cb = p.op_d("vector", lambda e, cb=cb: e.scalar_tensor_tensor(
                        xcf[:, cb * 512:(cb + 1) * 512], ab[:], 0.1,
                        xt[:, cb * 512:(cb + 1) * 512], ALU.mult, ALU.add),
                        [xt_tok, store_prev])
                mmy_last = mmy
                stt_prev = stt_cb
                store_prev = p.op_d("sync", lambda e, tile=tile: e.dma_start(
                    out=xo_ap[tile * 128:(tile + 1) * 128, :], in_=xcf[:]), [stt_cb], inc=16)



def emit_watt(p, st, ps, pa, x_ap, xo_ap):
    NW = 63
    wq_h, wk_h, wv_h, wo_h = st["Wq"], st["Wk"], st["Wv"], st["Wo"]
    bq_h, bk_h, bv_h, bo_h = st["bq"], st["bk"], st["bvr"], st["bor"]
    cnt_h, id_h = st["cnt"], st["idm"]
    q_s, k_s, xn_s = st["q_scr"], st["k_scr"], st["xn_scr"]
    if True:
        A = pa.t
        wsb = A([128, KC * DM], BF16)
        wo2 = A([128, KC * DM], BF16)
        stg = A([128, DM], F32)
        xnf = A([128, KC * 512], BF16)
        xt = A([128, 2 * DM], F32)
        xcf = A([128, 2 * DM], F32)
        lxt = A([128, DM], F32)
        lxcf = A([128, DM], F32)
        xnt = A([128, DM], BF16)
        qev = A([128, 2 * 512], BF16)
        cols = A([128, 8], F32)
        bcol = A([128, KC], F32)
        brow = A([1, 2 * DM], BF16)
        ones1 = A([1, 128], BF16)
        cnts = A([128, 3], F32)
        idm = A([128, 128], BF16)
        qw = A([128, 2 * KC * 128], BF16)
        kw = A([128, 2 * KC * 128], BF16)
        xw = A([128, 2 * KC * 128], BF16)
        vsb = A([128, 2 * DM], BF16)
        pbf = A([128, 2 * 128], BF16)
        ptb = A([128, 2 * 128], BF16)
        mcol = A([128, 4], F32)
        rinv = A([128, 2], F32)
        rrot = A([128, 2], F32)
        acc = A([128, 2 * DM], F32)
        anb = A([128, 2 * DM], BF16)
        afT = A([128, 2 * KC * 128], BF16)
        psBs = [ps["P2"], ps["P7"]]
        psAw = [ps["P1"], ps["P3"]]
        psS = ps["P5"]
        psT = ps["P6"]

        p.dma("sync", lambda e: e.dma_start(out=idm[:], in_=id_h[:]))
        p.dma("sync", lambda e: e.dma_start(out=cnts[:], in_=cnt_h[:]))
        p.op("vector", lambda e: e.memset(ones1[:], 1.0))
        p.dma("sync", lambda e: e.dma_start(out=stg[0:1, 0:DM], in_=bv_h[:]))
        p.op("vector", lambda e: e.tensor_copy(brow[:, 0:DM], stg[0:1, 0:DM]))
        p.dma("sync", lambda e: e.dma_start(out=stg[0:1, 0:DM], in_=bo_h[:]))
        p.op("vector", lambda e: e.tensor_copy(brow[:, DM:2 * DM], stg[0:1, 0:DM]))

        def load_w(w_h, dst):
            for k in range(KC):
                p.dma("sync", lambda e, k=k: e.dma_start(out=stg[:], in_=w_h[k * 128:(k + 1) * 128, :]))
                p.op("vector", lambda e, k=k: e.tensor_copy(dst[:, k * DM:(k + 1) * DM], stg[:]))

        def qk_phase(scr, with_ln):
            act_hist = [None, None]
            store_hist = [None, None]
            for nt in range(8):
                if with_ln:
                    for tq in range(4):
                        tile = nt * 4 + tq
                        emit_ln(p, x_ap, tile, lxt, lxcf, stg, xnt, cols)
                        emit_xnf_tile(p, xnt, xnf, tq, 512)
                    xnf_tok = p.dma_chain([
                        (lambda e, k=k, nt=nt: e.dma_start(
                            out=xn_s[:, k * T + nt * 512: k * T + nt * 512 + 512],
                            in_=xnf[:, k * 512:(k + 1) * 512]))
                        for k in range(KC)])
                else:
                    xnf_tok = p.dma_chain([
                        (lambda e, k=k, nt=nt: e.dma_start(
                            out=xnf[:, k * 512:(k + 1) * 512],
                            in_=xn_s[:, k * T + nt * 512: k * T + nt * 512 + 512]))
                        for k in range(KC)])
                for m in range(KC):
                    pq = m % 2
                    deps = [act_hist[pq]]
                    if m == 0:
                        deps.append(xnf_tok)
                    qmm = p.mm_chain_d([
                        (lambda e, k=k, m=m, pq=pq: e.matmul(
                            psAw[pq][:], wsb[:, k * DM + m * 128: k * DM + m * 128 + 128],
                            xnf[:, k * 512:(k + 1) * 512],
                            start=(k == 0), stop=(k == KC - 1)))
                        for k in range(KC)], deps)
                    act = p.op_d("scalar", lambda e, m=m, pq=pq: e.activation(
                        out=qev[:, pq * 512:(pq + 1) * 512], in_=psAw[pq][:],
                        func=AF.Identity, bias=bcol[:, m:m + 1]),
                        [qmm, store_hist[pq]])
                    store_hist[pq] = p.op_d("sync", lambda e, m=m, nt=nt, pq=pq: e.dma_start(
                        out=scr[:, m * T + nt * 512: m * T + nt * 512 + 512],
                        in_=qev[:, pq * 512:(pq + 1) * 512]), [act], inc=16)
                    act_hist[pq] = act

        load_w(wq_h, wsb)
        p.dma("sync", lambda e: e.dma_start(out=bcol[:], in_=bq_h[:]))
        qk_phase(q_s, with_ln=True)
        load_w(wk_h, wsb)
        p.dma("sync", lambda e: e.dma_start(out=bcol[:], in_=bk_h[:]))
        qk_phase(k_s, with_ln=False)

        load_w(wv_h, wsb)
        load_w(wo_h, wo2)
        p.op("vector", lambda e: e.memset(acc[:], 0.0))

        smm_hist = [None, None]
        omm_hist = [None, None]
        vcopy_hist = [None, None]
        exp_hist = None
        trans_hist = [None, None]
        ptcopy_hist = None
        sttr_hist = [None, None]
        anb_hist = [None, None]
        afr_hist = [None, None]
        add_hist = [None, None]
        store_hist = [None, None]
        ofmm_hist = [None, None]
        fin_state = {}

        def finalize(j, par):
            pj = j % 2
            a_sl = acc[:, par * DM:(par + 1) * DM]
            ci = 0 if j == 0 else (2 if j == NTILE - 1 else 1)
            anb_mul = p.op_d("vector", lambda e, a_sl=a_sl, ci=ci, pj=pj: e.tensor_scalar_mul(
                anb[:, pj * DM:(pj + 1) * DM], a_sl, cnts[:, ci:ci + 1]), [afr_hist[pj]])
            aft = p.dma_chain_d([
                (lambda e, k=k, pj=pj: e.dma_start_transpose(
                    out=afT[:, pj * KC * 128 + k * 128: pj * KC * 128 + (k + 1) * 128],
                    in_=anb[:, pj * DM + k * 128: pj * DM + (k + 1) * 128]))
                for k in range(KC)], [anb_mul, ofmm_hist[pj]])
            xt_tok = p.op_d("sync", lambda e, j=j, pj=pj: e.dma_start(
                out=xt[:, pj * DM:(pj + 1) * DM], in_=x_ap[j * 128:(j + 1) * 128, :]),
                [add_hist[pj]], inc=16)
            addv = None
            last_ofmm = None
            for cb in range(2):
                ofmm = p.mm_chain_d(
                    [(lambda e, k=k, cb=cb, pj=pj: e.matmul(
                        psAw[cb][:], afT[:, pj * KC * 128 + k * 128: pj * KC * 128 + (k + 1) * 128],
                        wo2[:, k * DM + cb * 512: k * DM + cb * 512 + 512],
                        start=(k == 0), stop=False))
                     for k in range(KC)] +
                    [(lambda e, cb=cb: e.matmul(
                        psAw[cb][:], ones1[:], brow[0:1, DM + cb * 512: DM + cb * 512 + 512],
                        start=False, stop=True))], [aft, fin_state.get(("addh", cb))])
                addv = p.op_d("vector", lambda e, cb=cb, pj=pj: e.tensor_add(
                    xcf[:, pj * DM + cb * 512: pj * DM + cb * 512 + 512],
                    xt[:, pj * DM + cb * 512: pj * DM + cb * 512 + 512], psAw[cb][:]),
                    [ofmm, xt_tok, store_hist[pj]])
                fin_state[("addh", cb)] = addv
                last_ofmm = ofmm
            store_hist[pj] = p.op_d("sync", lambda e, j=j, pj=pj: e.dma_start(
                out=xo_ap[j * 128:(j + 1) * 128, :], in_=xcf[:, pj * DM:(pj + 1) * DM]),
                [addv], inc=16)
            clr = p.op_d("vector", lambda e, a_sl=a_sl: e.memset(a_sl, 0.0), [])
            afr_hist[pj] = clr
            add_hist[pj] = addv
            ofmm_hist[pj] = last_ofmm

        for w in range(NW):
            start = 64 * w
            pw = w % 2
            qo = pw * KC * 128
            ld = p.dma_chain_d(
                [(lambda e, k=k, start=start, qo=qo: e.dma_start(
                    out=qw[:, qo + k * 128: qo + (k + 1) * 128], in_=q_s[:, k * T + start: k * T + start + 128]))
                 for k in range(KC)] +
                [(lambda e, k=k, start=start, qo=qo: e.dma_start(
                    out=kw[:, qo + k * 128: qo + (k + 1) * 128], in_=k_s[:, k * T + start: k * T + start + 128]))
                 for k in range(KC)] +
                [(lambda e, k=k, start=start, qo=qo: e.dma_start(
                    out=xw[:, qo + k * 128: qo + (k + 1) * 128], in_=xn_s[:, k * T + start: k * T + start + 128]))
                 for k in range(KC)], [smm_hist[pw]])
            for cb in range(2):
                vmm = p.mm_chain_d(
                    [(lambda e, k=k, cb=cb, qo=qo: e.matmul(
                        psBs[cb][:], xw[:, qo + k * 128: qo + (k + 1) * 128],
                        wsb[:, k * DM + cb * 512: k * DM + cb * 512 + 512],
                        start=(k == 0), stop=False))
                     for k in range(KC)] +
                    [(lambda e, cb=cb: e.matmul(
                        psBs[cb][:], ones1[:], brow[0:1, cb * 512: cb * 512 + 512],
                        start=False, stop=True))], [ld, vcopy_hist[cb], sttr_hist[cb]])
                vcopy_hist[cb] = p.op_d("vector", lambda e, cb=cb, pw=pw: e.tensor_copy(
                    vsb[:, pw * DM + cb * 512: pw * DM + cb * 512 + 512], psBs[cb][:]),
                    [vmm, omm_hist[pw]])
            smm = p.mm_chain_d([
                (lambda e, k=k, qo=qo: e.matmul(
                    psS[:], qw[:, qo + k * 128: qo + (k + 1) * 128], kw[:, qo + k * 128: qo + (k + 1) * 128],
                    start=(k == 0), stop=(k == KC - 1)))
                for k in range(KC)], [ld, exp_hist])
            rmax = p.op_d("vector", lambda e, pw=pw: e.tensor_reduce(
                mcol[:, pw * 2: pw * 2 + 1], psS[:], AX.X, ALU.max), [smm])
            mscale = p.op_d("vector", lambda e, pw=pw: e.tensor_scalar_mul(
                mcol[:, pw * 2 + 1: pw * 2 + 2], mcol[:, pw * 2: pw * 2 + 1], -0.125), [])
            expv = p.op_d("scalar", lambda e, pw=pw: e.activation(
                out=pbf[:, pw * 128:(pw + 1) * 128], in_=psS[:], func=AF.Exp,
                bias=mcol[:, pw * 2 + 1: pw * 2 + 2], scale=0.125),
                [smm, mscale, trans_hist[pw]])
            p.op_d("vector", lambda e, pw=pw: e.tensor_reduce(
                mcol[:, pw * 2: pw * 2 + 1], pbf[:, pw * 128:(pw + 1) * 128], AX.X, ALU.add), [expv])
            rinvv = p.op_d("vector", lambda e, pw=pw: e.reciprocal(
                rinv[:, pw:pw + 1], mcol[:, pw * 2: pw * 2 + 1]), [])
            trans = p.op_d("tensor", lambda e, pw=pw: e.transpose(
                psT[:], pbf[:, pw * 128:(pw + 1) * 128], idm[:]), [expv, ptcopy_hist])
            ptcopy_hist = p.op_d("vector", lambda e, pw=pw: e.tensor_copy(
                ptb[:, pw * 128:(pw + 1) * 128], psT[:]), [trans, omm_hist[pw]])
            trans_hist[pw] = trans
            exp_hist = expv
            j = w // 2
            par = j % 2
            last_omm = None
            if w % 2 == 0:
                for cb in range(2):
                    omm = p.op_d("tensor", lambda e, cb=cb, pw=pw: e.matmul(
                        psBs[cb][:], ptb[:, pw * 128:(pw + 1) * 128],
                        vsb[:, pw * DM + cb * 512: pw * DM + cb * 512 + 512],
                        start=True, stop=True), [ptcopy_hist, vcopy_hist[cb], sttr_hist[cb]])
                    sttr_hist[cb] = p.op_d("vector", lambda e, cb=cb, par=par, pw=pw: e.scalar_tensor_tensor(
                        acc[:, par * DM + cb * 512: par * DM + cb * 512 + 512],
                        psBs[cb][:], rinv[:, pw:pw + 1],
                        acc[:, par * DM + cb * 512: par * DM + cb * 512 + 512],
                        ALU.mult, ALU.add), [omm, rinvv])
                    last_omm = omm
            else:
                rr1 = p.op_d("sync", lambda e, pw=pw: e.dma_start(
                    out=rrot[64:128, pw:pw + 1], in_=rinv[0:64, pw:pw + 1]), [rinvv], inc=16)
                rr2 = p.op_d("sync", lambda e, pw=pw: e.dma_start(
                    out=rrot[0:64, pw:pw + 1], in_=rinv[64:128, pw:pw + 1]), [rinvv], inc=16)
                for cb in range(2):
                    p.op_d("tensor", lambda e, cb=cb, pw=pw: e.matmul(
                        psBs[cb][0:64, :], ptb[:, pw * 128 + 64: pw * 128 + 128],
                        vsb[:, pw * DM + cb * 512: pw * DM + cb * 512 + 512],
                        start=True, stop=True), [ptcopy_hist, vcopy_hist[cb], sttr_hist[cb]])
                    omm = p.op_d("tensor", lambda e, cb=cb, pw=pw: e.matmul(
                        psBs[cb][64:128, :], ptb[:, pw * 128: pw * 128 + 64],
                        vsb[:, pw * DM + cb * 512: pw * DM + cb * 512 + 512],
                        start=True, stop=True), [])
                    s1 = p.op_d("vector", lambda e, cb=cb, par=par, pw=pw: e.scalar_tensor_tensor(
                        acc[64:128, par * DM + cb * 512: par * DM + cb * 512 + 512],
                        psBs[cb][64:128, :], rrot[64:128, pw:pw + 1],
                        acc[64:128, par * DM + cb * 512: par * DM + cb * 512 + 512],
                        ALU.mult, ALU.add), [omm, rr1, rr2])
                    sttr_hist[cb] = p.op_d("vector", lambda e, cb=cb, par=par, pw=pw: e.scalar_tensor_tensor(
                        acc[0:64, (1 - par) * DM + cb * 512: (1 - par) * DM + cb * 512 + 512],
                        psBs[cb][0:64, :], rrot[0:64, pw:pw + 1],
                        acc[0:64, (1 - par) * DM + cb * 512: (1 - par) * DM + cb * 512 + 512],
                        ALU.mult, ALU.add), [])
                    last_omm = omm
                finalize(j, par)
            omm_hist[pw] = last_omm
            smm_hist[pw] = smm
        finalize(NTILE - 1, (NTILE - 1) % 2)



def build_fused():
    nc = bass.Bass("TRN2", target_bir_lowering=False)
    st = {}
    st["x"] = nc.dram_tensor("x", [T, DM], F32, kind="ExternalInput")
    st["xo"] = nc.dram_tensor("xo", [T, DM], F32, kind="ExternalOutput")
    for i in range(1, 6):
        st[f"xi{i}"] = nc.dram_tensor(f"xi{i}", [T, DM], F32, kind="Internal")
    st["W1"] = nc.dram_tensor("W1", [8, DM, 1536], F32, kind="ExternalInput")
    st["W2"] = nc.dram_tensor("W2", [8, 1536, DM], F32, kind="ExternalInput")
    st["W3"] = nc.dram_tensor("W3", [8, DM, 1536], F32, kind="ExternalInput")
    st["Wr"] = nc.dram_tensor("Wr", [DM, 8], F32, kind="ExternalInput")
    st["w1b"] = nc.dram_tensor("w1b", [8, DM, 1536], BF16, kind="Internal")
    st["w2b"] = nc.dram_tensor("w2b", [8, 1536, DM], BF16, kind="Internal")
    st["w3b"] = nc.dram_tensor("w3b", [8, DM, 1536], BF16, kind="Internal")
    st["xn_scr"] = nc.dram_tensor("xn_scr", [128, KC * T], BF16, kind="Internal")
    st["k123T"] = nc.dram_tensor("k123T", [DM, 96], F32, kind="ExternalInput")
    st["abc"] = nc.dram_tensor("abc", [128, 48], F32, kind="ExternalInput")
    st["vwT"] = nc.dram_tensor("vwT", [DM, DM], F32, kind="ExternalInput")
    st["owT"] = nc.dram_tensor("owT", [DM, DM], F32, kind="ExternalInput")
    st["nb128"] = nc.dram_tensor("nb128", [128, T], F32, kind="ExternalInput")
    st["selB"] = nc.dram_tensor("selB", [16, KC * 128], F32, kind="ExternalInput")
    st["blk16"] = nc.dram_tensor("blk16", [128, KC * 16], BF16, kind="ExternalInput")
    st["WfcT"] = nc.dram_tensor("WfcT", [DM, 4096], F32, kind="ExternalInput")
    st["WprT"] = nc.dram_tensor("WprT", [4096, DM], F32, kind="ExternalInput")
    st["Wq"] = nc.dram_tensor("Wq", [DM, DM], F32, kind="ExternalInput")
    st["Wk"] = nc.dram_tensor("Wk", [DM, DM], F32, kind="ExternalInput")
    st["Wv"] = nc.dram_tensor("Wv", [DM, DM], F32, kind="ExternalInput")
    st["Wo"] = nc.dram_tensor("Wo", [DM, DM], F32, kind="ExternalInput")
    st["bq"] = nc.dram_tensor("bq", [128, KC], F32, kind="ExternalInput")
    st["bk"] = nc.dram_tensor("bk", [128, KC], F32, kind="ExternalInput")
    st["bvr"] = nc.dram_tensor("bvr", [1, DM], F32, kind="ExternalInput")
    st["bor"] = nc.dram_tensor("bor", [1, DM], F32, kind="ExternalInput")
    st["cnt"] = nc.dram_tensor("cnt", [128, 3], F32, kind="ExternalInput")
    st["idm"] = nc.dram_tensor("idm", [128, 128], BF16, kind="ExternalInput")
    st["q_scr"] = nc.dram_tensor("q_scr", [128, KC * T], BF16, kind="Internal")
    st["k_scr"] = nc.dram_tensor("k_scr", [128, KC * T], BF16, kind="Internal")

    pool_bf = nc.alloc_sbuf_tensor("pool", [128, POOL_BYTES // 2], BF16)
    pool_f32 = pool_bf.bitcast(F32)
    ps = {
        "P1": nc.psum_tensor("P1", [128, 512], F32).__enter__(),
        "P2": nc.psum_tensor("P2", [128, 512], F32).__enter__(),
        "P3": nc.psum_tensor("P3", [128, 512], F32).__enter__(),
        "P4": nc.psum_tensor("P4", [128, 512], F32).__enter__(),
        "P5": nc.psum_tensor("P5", [128, 128], F32).__enter__(),
        "P6": nc.psum_tensor("P6", [128, 128], BF16).__enter__(),
        "P7": nc.psum_tensor("P7", [128, 512], F32).__enter__(),
        "P8": nc.psum_tensor("P8", [128, 512], F32).__enter__(),
    }
    p = SeqProg(nc)

    def pa():
        return PoolAlloc(pool_bf, pool_f32)

    p.serial_mode = not DEP_MOE
    emit_moe(p, st, ps, pa(), st["x"], st["xi1"], cast_weights=True)
    p.serial_mode = False
    emit_lat(p, st, ps, pa(), st["xi1"], st["xi2"])
    p.serial_mode = not DEP_MOE
    emit_moe(p, st, ps, pa(), st["xi2"], st["xi3"], cast_weights=False)
    p.serial_mode = not DEP_SFFN
    emit_sffn(p, st, ps, pa(), st["xi3"], st["xi4"])
    p.serial_mode = False
    emit_watt(p, st, ps, pa(), st["xi4"], st["xi5"])
    p.serial_mode = not DEP_SFFN
    emit_sffn(p, st, ps, pa(), st["xi5"], st["xo"])
    p.serial_mode = False
    p.emit()
    return nc



def _host_prep(inputs):
    d = {}
    k123T = np.zeros((DM, 96), np.float32)
    k123T[:, 0:16] = inputs["L0_attn_k1"].T
    k123T[:, 32:48] = inputs["L0_attn_k2"].T
    k123T[:, 64:80] = inputs["L0_attn_k3"].T
    d["k123T"] = k123T
    abc = np.zeros((6, KC, 128), np.float32)
    for nm, idx in (("a1", 0), ("b1", 1), ("a2", 2), ("b2", 3), ("c", 4)):
        abc[idx] = inputs["L0_attn_" + nm][:, :, 0].reshape(KC, 128)
    d["abc"] = np.ascontiguousarray(abc.transpose(2, 0, 1).reshape(128, 48))
    d["vwT"] = np.ascontiguousarray(
        inputs["L0_attn_value_weight"].transpose(1, 0, 2).reshape(DM, DM))
    d["owT"] = np.ascontiguousarray(
        inputs["L0_attn_output_weight"].transpose(0, 2, 1).reshape(DM, DM))
    nlin = np.linspace(0.0, 1.0, T, dtype=np.float32)
    d["nb128"] = np.broadcast_to(nlin[None, :], (128, T)).copy()
    selB = np.zeros((KC, 16, 128), np.float32)
    blk16 = np.zeros((KC, 128, 16), np.float32)
    for hp in range(KC):
        for r in range(2):
            selB[hp, 2 * hp + r, 64 * r:64 * (r + 1)] = 1.0
            blk16[hp, 64 * r:64 * (r + 1), 2 * hp + r] = 1.0
    d["selB"] = np.ascontiguousarray(selB.transpose(1, 0, 2).reshape(16, KC * 128))
    d["blk16"] = np.ascontiguousarray(blk16.transpose(1, 0, 2).reshape(128, KC * 16)).astype(BF)
    d["WfcT"] = np.ascontiguousarray(inputs["L1_ffn_W_fc"].T)
    d["WprT"] = np.ascontiguousarray(inputs["L1_ffn_W_proj"].T)
    d["bq"] = np.ascontiguousarray(inputs["L1_attn_bq"].reshape(KC, 128).T)
    d["bk"] = np.ascontiguousarray(inputs["L1_attn_bk"].reshape(KC, 128).T)
    d["bvr"] = inputs["L1_attn_bv"].reshape(1, DM).copy()
    d["bor"] = inputs["L1_attn_bo"].reshape(1, DM).copy()
    cnt = np.full((3, 128), 0.5, np.float32)
    cnt[0, 0:64] = 1.0
    cnt[2, 64:128] = 1.0
    d["cnt"] = np.ascontiguousarray(cnt.T)
    d["idm"] = np.eye(128).astype(BF)
    return d


_PROGS = {}


def _get_fused():
    if "fused" not in _PROGS:
        _PROGS["fused"] = build_fused()
    return _PROGS["fused"]


def kernel(**inputs):
    hp = _host_prep(inputs)
    B = 8
    x = np.ascontiguousarray(inputs["L0_attn_x"].astype(np.float32))
    extra = {"W1": inputs["L0_ffn_W1"], "W2": inputs["L0_ffn_W2"],
             "W3": inputs["L0_ffn_W3"], "Wr": inputs["L0_ffn_Wr"],
             "k123T": hp["k123T"], "abc": hp["abc"], "vwT": hp["vwT"],
             "owT": hp["owT"], "nb128": hp["nb128"], "selB": hp["selB"],
             "blk16": hp["blk16"],
             "WfcT": hp["WfcT"], "WprT": hp["WprT"],
             "Wq": inputs["L1_attn_Wq"], "Wk": inputs["L1_attn_Wk"],
             "Wv": inputs["L1_attn_Wv"], "Wo": inputs["L1_attn_Wo"],
             "bq": hp["bq"], "bk": hp["bk"], "bvr": hp["bvr"],
             "bor": hp["bor"], "cnt": hp["cnt"], "idm": hp["idm"]}
    in_maps = []
    for b in range(B):
        m = {"x": np.ascontiguousarray(x[b])}
        m.update(extra)
        in_maps.append(m)
    res = run_bass_kernel_spmd(_get_fused(), in_maps, core_ids=list(range(B)))
    return np.stack([np.asarray(r["xo"]) for r in res.results], axis=0)
